# version 09 (final)
# baseline (speedup 1.0000x reference)
import numpy as np
import ml_dtypes
import concourse.bass as bass
import concourse.mybir as mybir
import bass_rust
from concourse.bass_utils import run_bass_kernel_spmd
from contextlib import ExitStack

F32 = mybir.dt.float32
F32R = mybir.dt.float32r
BF16 = mybir.dt.bfloat16
U8 = mybir.dt.uint8
AF = mybir.ActivationFunctionType
ALU = mybir.AluOpType

D = 2048
TL = 1024
NT = 16
NKA = 44
NKAP = 64
EPS = 1e-5
INV_D = 1.0 / 2048.0
RG = [[0, 1], [2, 3], [4, 5], [6, 7]]
ENGS = ("sync", "tensor", "vector", "scalar", "gpsimd")


def _had(n):
    h = np.array([[1.0]], dtype=np.float64)
    while h.shape[0] < n:
        h = np.block([[h, h], [h, -h]])
    return h


class Sem:
    __slots__ = ("h", "n")

    def __init__(self):
        self.h = None
        self.n = 0


class Plan:
    def __init__(self):
        self.q = {e: [] for e in ENGS}

    def w(self, eng, sem, val):
        if val and val > 0:
            self.q[eng].append(lambda e, s=sem, v=val: e.wait_ge(s.h, v))

    def ws(self, eng, deps):
        for d in deps:
            if d is not None:
                self.w(eng, d[0], d[1])

    def o(self, eng, fn, inc=None, k=1):
        if inc is None:
            self.q[eng].append(fn)
            return None
        inc.n += k
        val = inc.n
        self.q[eng].append(lambda e, f=fn, s=inc, kk=k: f(e).then_inc(s.h, kk))
        return val


def build_nc():
    nc = bass.Bass("TRN2", target_bir_lowering=False, debug=False)

    def din(name, shape, dt=F32):
        return nc.dram_tensor(name, list(shape), dt, kind="ExternalInput").ap()

    def dint(name, shape, dt=F32):
        return nc.dram_tensor(name, list(shape), dt, kind="Internal").ap()

    x0_h = din("x0", [NT, 128, TL])
    iwp_h = din("iwp", [33, 16, 128, 128])
    dtp_h = din("dtp", [8, 128, 128])
    dtb_h = din("dtb", [128, 8])
    alog_h = din("alog", [128, 256])
    dcol_h = din("dcol", [128, 8])
    opj_h = din("opj", [16, 8, 128, 128])
    fup_h = din("fup", [64, 16, 128, 128])
    fga_h = din("fga", [64, 16, 128, 128])
    wdt_h = din("wdt", [64, 128, D])
    agu_h = din("agu", [44, 16, 128, 128])
    auu_h = din("auu", [44, 16, 128, 128])
    adn_h = din("adn", [16, 44, 128, 128])
    hp_h = din("hp", [128, 128], BF16)
    hn_h = din("hn", [128, 128], BF16)
    hd_h = din("hd", [128, 128], BF16)
    eall_h = din("eall", [128, 32 * 128], F32R)
    onec_h = din("onec", [128, 1], F32R)
    oner_h = din("oner", [1, 128], F32R)
    sucol_h = din("sucol", [128, 16])
    wsu_h = din("wsu", [128, 64])
    wsg_h = din("wsg", [128, 64])
    sud_h = din("sud", [128, 64])
    wsd_h = din("wsd", [128, 16])
    n1c_h = din("n1c", [128, 16])
    n2c_h = din("n2c", [128, 16])
    cmisc_h = din("cmisc", [128, 4])

    out_h = nc.dram_tensor("out", [NT, 128, TL], F32, kind="ExternalOutput").ap()

    x1d = dint("x1d", [NT, 128, TL])
    x2d = dint("x2d", [NT, 128, TL])
    unin = dint("unin", [NT, 128, TL], BF16)
    unout = dint("unout", [2, NT, 128, TL], BF16)
    rsin = dint("rsin", [2, NT, 128, TL])
    rsout = dint("rsout", [NT, 128, TL])
    wdhd = dint("wdhd", [16, 64, 128, 128], BF16)
    iwpd = dint("iwpd", [33, 16, 128, 128], BF16)
    dtpd = dint("dtpd", [8, 128, 128], BF16)
    opjd = dint("opjd", [16, 8, 128, 128], BF16)
    fupd = dint("fupd", [64, 16, 128, 128], BF16)
    fgad = dint("fgad", [64, 16, 128, 128], BF16)

    es = ExitStack()
    with es:
        def S(name, shape, dt=F32):
            return es.enter_context(nc.sbuf_tensor(name, list(shape), dt))

        c_hp = S("c_hp", [128, 128], BF16)
        c_hn = S("c_hn", [128, 128], BF16)
        c_hd = S("c_hd", [128, 128], BF16)
        c_eall = S("c_eall", [128, 4096], F32R)
        c_onec = S("c_onec", [128, 1], F32R)
        c_oner = S("c_oner", [1, 128], F32R)
        b_dtb = S("b_dtb", [128, 8])
        b_acol = S("b_acol", [128, 256])
        b_dcol = S("b_dcol", [128, 8])
        b_su = S("b_su", [128, 16])
        b_wsu = S("b_wsu", [128, 64])
        b_wsg = S("b_wsg", [128, 64])
        b_sud = S("b_sud", [128, 64])
        b_wsd = S("b_wsd", [128, 16])
        b_n1 = S("b_n1", [128, 16])
        b_n2 = S("b_n2", [128, 16])
        b_wsue = S("b_wsue", [128, 64])
        b_wsge = S("b_wsge", [128, 64])
        b_wsde = S("b_wsde", [128, 16])
        c_misc = S("c_misc", [128, 4])
        rows = S("rows", [1, 4096], F32R)
        hbuf = S("hbuf", [128, 1024])
        carry = S("carry", [128, 256])
        szb = S("szb", [128, 512])
        wpanA = S("wpanA", [128, 2048], BF16)
        wpanB = S("wpanB", [128, 2048], BF16)
        wpanC = S("wpanC", [128, 2048], BF16)
        statr = S("statr", [128, 1024], F32R)
        ctr = S("ctr", [128, 512], F32R)
        btr = S("btr", [128, 512], F32R)
        a1 = S("a1", [128, 65536], U8)
        a2 = S("a2", [128, 32768], U8)
        a3 = S("a3", [128, 49152], U8)

        p0 = es.enter_context(nc.psum_tensor("p0", [128, 512], F32))
        p1 = es.enter_context(nc.psum_tensor("p1", [128, 512], F32))
        p2 = es.enter_context(nc.psum_tensor("p2", [128, 512], F32))
        p3 = es.enter_context(nc.psum_tensor("p3", [128, 512], F32))
        pB = es.enter_context(nc.psum_tensor("pB", [128, 1024], F32))
        pC = es.enter_context(nc.psum_tensor("pC", [128, 1024], F32))
        PSV = {"p0": p0[:, :], "p1": p1[:, :], "p2": p2[:, :], "p3": p3[:, :],
               "pB0": pB[:, 0:512], "pB1": pB[:, 512:1024],
               "pC0": pC[:, 0:512], "pC1": pC[:, 512:1024]}

        sems = {n: Sem() for n in ("sLD", "sST", "sWC", "sCC", "sTE", "sDV",
                                   "sSC", "sGP", "sS2", "sT2")}
        for n, s in sems.items():
            s.h = es.enter_context(nc.semaphore(n))
        sLD = sems["sLD"]; sST = sems["sST"]; sWC = sems["sWC"]
        sCC = sems["sCC"]; sTE = sems["sTE"]; sDV = sems["sDV"]
        sSC = sems["sSC"]; sGP = sems["sGP"]
        sS2 = sems["sS2"]; sT2 = sems["sT2"]

        nc.const_aps.register(F32, EPS, c_misc[:, 0:1])
        nc.const_aps.register(F32, 1.0, c_misc[:, 1:2])
        nc.const_aps.register(F32, -0.5, c_misc[:, 2:3])
        nc.const_aps.register(F32, -1.0, c_misc[:, 3:4])

        pl = Plan()
        pfree = {}

        def av(arena, off_bytes, dt, rows_, cols):
            bs = {F32: 4, F32R: 4, BF16: 2, U8: 1}[dt]
            return arena[0:rows_, off_bytes:off_bytes + cols * bs].bitcast(dt)

        def ld(dst, src, q="sync"):
            sem = {"sync": sLD, "gpsimd": sWC, "scalar": sS2}[q]
            return pl.o(q, lambda e, d=dst, s=src: e.dma_start(out=d, in_=s),
                        inc=sem, k=16)

        def st(dst, src, q="sync"):
            sem = sST if q == "sync" else sT2
            return pl.o(q, lambda e, d=dst, s=src: e.dma_start(out=d, in_=s),
                        inc=sem, k=16)

        def mm_group(pk, pairs, waits=(), mid_waits=None, out=None):
            ps = out if out is not None else PSV[pk]
            deps = list(waits)
            if pk in pfree:
                deps.append(pfree[pk])
            n = len(pairs)
            v = None
            for i, (lt, rh) in enumerate(pairs):
                if i == 0:
                    pl.ws("tensor", deps)
                if mid_waits and i in mid_waits:
                    pl.ws("tensor", mid_waits[i])
                if i == n - 1:
                    v = pl.o("tensor", lambda e, o=ps, l=lt, r=rh, s=(i == 0):
                             e.matmul(o, l, r, start=s, stop=True), inc=sTE)
                else:
                    pl.o("tensor", lambda e, o=ps, l=lt, r=rh, s=(i == 0):
                         e.matmul(o, l, r, start=s, stop=False))
            return (sTE, v)

        def barrier():
            vals = [(s, s.n) for s in sems.values()]
            for eng in ENGS:
                for (s, v) in vals:
                    pl.w(eng, s, v)

        def rslot(i):
            return rows[0:1, i * 512:(i + 1) * 512]

        def row_math_ln(s1_row, s2_row, m_slot, r_slot, comb_slot=None,
                        dep=None, rms_only=False):
            m = rslot(m_slot) if m_slot is not None else None
            r = rslot(r_slot)
            sc3 = rslot(0)
            pl.ws("vector", [dep] if dep else [])
            if rms_only:
                vv = pl.o("vector", lambda e: e.tensor_scalar(
                    r, s2_row, INV_D, None, ALU.mult), inc=sDV)
                pl.w("scalar", sDV, vv)
                pl.o("scalar", lambda e: e.activation(sc3, r, AF.Ln,
                                                      bias=EPS, scale=1.0))
                vs = pl.o("scalar", lambda e: e.activation(
                    r, sc3, AF.Exp, bias=0.0, scale=-0.5), inc=sSC)
                return (sSC, vs)
            pl.o("vector", lambda e: e.tensor_scalar(m, s1_row, INV_D, None,
                                                     ALU.mult))
            pl.o("vector", lambda e: e.tensor_tensor(sc3, m, m, ALU.mult))
            vv = pl.o("vector", lambda e: e.scalar_tensor_tensor(
                r, s2_row, INV_D, sc3, ALU.mult, ALU.subtract), inc=sDV)
            pl.w("scalar", sDV, vv)
            pl.o("scalar", lambda e: e.activation(sc3, r, AF.Ln, bias=EPS,
                                                  scale=1.0))
            if comb_slot is None:
                vs = pl.o("scalar", lambda e: e.activation(
                    r, sc3, AF.Exp, bias=0.0, scale=-0.5), inc=sSC)
                return (sSC, vs)
            cb = rslot(comb_slot)
            pl.o("scalar", lambda e: e.activation(r, sc3, AF.Exp, bias=0.0,
                                                  scale=-0.5))
            vs = pl.o("scalar", lambda e: e.activation(cb, sc3, AF.Exp,
                                                       bias=0.0, scale=-1.0),
                      inc=sSC)
            pl.w("vector", sSC, vs)
            pl.o("vector", lambda e: e.tensor_tensor(sc3, m, m, ALU.mult))
            pl.o("vector", lambda e: e.scalar_tensor_tensor(
                sc3, s2_row, INV_D, sc3, ALU.mult, ALU.subtract))
            vv2 = pl.o("vector", lambda e: e.tensor_tensor(cb, sc3, cb,
                                                           ALU.mult), inc=sDV)
            pl.w("scalar", sDV, vv2)
            pl.o("scalar", lambda e: e.activation(cb, cb, AF.Ln, bias=EPS,
                                                  scale=1.0))
            vs2 = pl.o("scalar", lambda e: e.activation(cb, cb, AF.Exp,
                                                        bias=0.0, scale=-0.5),
                       inc=sSC)
            pl.w("vector", sSC, vs2)
            vv3 = pl.o("vector", lambda e: e.tensor_tensor(cb, r, cb,
                                                           ALU.mult), inc=sDV)
            return (sDV, vv3)

        def bcast2(rowA, rowB, dep):
            dA = mm_group("pB0", [(c_oner[0:1, :], rowA)], waits=[dep])
            dB = mm_group("pB1", [(c_oner[0:1, :], rowB)], waits=[dep])
            return dA, dB

        for dst, src in [(c_hp, hp_h), (c_hn, hn_h), (c_hd, hd_h),
                         (c_eall, eall_h), (c_onec, onec_h), (c_oner, oner_h),
                         (b_dtb, dtb_h), (b_acol, alog_h), (b_dcol, dcol_h),
                         (b_su, sucol_h), (b_wsu, wsu_h), (b_wsg, wsg_h),
                         (b_sud, sud_h), (b_wsd, wsd_h), (b_n1, n1c_h),
                         (b_n2, n2c_h), (c_misc, cmisc_h)]:
            ld(dst[:, :], src[:, :])
        v_cl = sLD.n
        pl.w("scalar", sLD, v_cl)
        pl.o("scalar", lambda e: e.activation(b_acol[:, :], b_acol[:, :],
                                              AF.Exp, bias=0.0, scale=1.0))
        pl.o("scalar", lambda e: e.mul(b_acol[:, :], b_acol[:, :], -1.0),
             inc=sSC)
        pl.w("vector", sLD, v_cl)
        pl.o("vector", lambda e: e.tensor_scalar(
            b_wsge[:, :], b_wsg[:, :], float(1.0 / np.sqrt(2048.0)), None,
            ALU.mult))
        pl.o("vector", lambda e: e.scalar_tensor_tensor(
            b_wsue[:, :], b_wsu[:, :], float(1.0 / np.sqrt(2048.0)),
            b_sud[:, :], ALU.mult, ALU.mult))
        pl.o("vector", lambda e: e.tensor_scalar(
            b_wsde[:, :], b_wsd[:, :], float(0.5 / np.sqrt(8192.0)), None,
            ALU.mult), inc=sDV)

        pS_items = [(iwp_h[i].rearrange("a p b -> p a b"),
                     iwpd[i].rearrange("a p b -> p a b"), 2048)
                    for i in range(33)]
        pG_items = ([(opj_h[i].rearrange("a p b -> p a b"),
                      opjd[i].rearrange("a p b -> p a b"), 1024)
                     for i in range(16)]
                    + [(dtp_h[i], dtpd[i], 128) for i in range(8)])
        castS = [0, 0]; stS = [0, 0]; stG = [0, 0]
        mmW = [0, 0]; stW = [0, 0]
        si = gi = 0
        for i1 in range(64):
            par = i1 % 2
            pl.w("gpsimd", sTE, mmW[par])
            lv1 = ld(wpanA[:, par * 1024:(par + 1) * 1024],
                     wdt_h[i1, :, 0:1024], q="gpsimd")
            lv2 = ld(wpanB[:, par * 1024:(par + 1) * 1024],
                     wdt_h[i1, :, 1024:2048], q="gpsimd")
            evd = []
            for half, pan in ((0, wpanA), (1, wpanB)):
                for oc in range(2):
                    pk = "p0" if (half * 2 + oc) % 2 == 0 else "p1"
                    dep = mm_group(pk, [(c_hp[:, :].bitcast(BF16),
                                         pan[:, par * 1024 + oc * 512:
                                             par * 1024 + (oc + 1) * 512])],
                                   waits=[(sWC, lv2), (sLD, v_cl),
                                          (sST, stW[par])])
                    dst = av(a1, par * 4096 + (half * 1024 + oc * 512) * 2,
                             BF16, 128, 512)
                    pl.ws("vector", [dep])
                    vv = pl.o("vector", lambda e, d=dst, s=PSV[pk]:
                              e.tensor_copy(d, s), inc=sDV)
                    pfree[pk] = (sDV, vv)
                    evd.append(vv)
            mmW[par] = dep[1]
            stage3 = av(a1, par * 4096, BF16, 128, 2048).rearrange(
                "p (a b) -> p a b", a=16, b=128)
            dstap = bass_rust.AP(wdhd.tensor, i1 * 128,
                                 [[8192, 128], [64 * 128 * 128, 16], [1, 128]])
            pl.w("sync", sDV, evd[-1])
            stW[par] = st(dstap, stage3)
            if si < len(pS_items):
                j = si % 2
                srcS, dstS, colS = pS_items[si]
                stgF = av(a3, j * 8192, F32, 128, colS)
                stgB = av(a2, j * 4096, BF16, 128, colS)
                pl.w("sync", sSC, castS[j])
                lvS = ld(stgF, srcS)
                pl.ws("scalar", [(sLD, lvS), (sST, stS[j])])
                castS[j] = pl.o("scalar", lambda e, d=stgB, s=stgF:
                                e.activation(d, s, AF.Copy, bias=0.0,
                                             scale=1.0), inc=sSC)
                pl.w("sync", sSC, castS[j])
                stS[j] = st(dstS, stgB)
                si += 1
            if i1 % 2 == 1 and gi < len(pG_items):
                j = gi % 2
                srcG, dstG, colG = pG_items[gi]
                stgG = av(a2, 8192 + j * 4096, BF16, 128, colG)
                pl.w("gpsimd", sST, stG[j])
                lvG = ld(stgG, srcG, q="gpsimd")
                pl.w("sync", sWC, lvG)
                stG[j] = st(dstG, stgG)
                gi += 1
        v_wdh = sST.n

        barrier()

        H16S = np.sign(_had(16)).astype(int)

        def ffn(xin_d, xout_d, ffn1, m_base, r_base):
            t1v = av(a3, 2048, F32, 128, 512)
            for sc in range(2):
                tsl = slice(sc * 512, (sc + 1) * 512)
                xr = statr[:, 0:512]
                xq = statr[:, 512:1024]
                if ffn1:
                    for kt in range(NT):
                        xt = av(a3, 4096 + (kt % 2) * 2048, F32, 128, 512)
                        pl.w("sync", sDV, sDV.n)
                        pl.w("sync", sSC, sSC.n)
                        lv = ld(xt, xin_d[kt, :, tsl])
                        pl.w("vector", sLD, lv)
                        if kt > 0:
                            pl.w("vector", sTE, sTE.n)
                        vv = pl.o("vector", lambda e, s=xt: e.tensor_copy(xr, s),
                                  inc=sDV)
                        pl.w("scalar", sLD, lv)
                        if kt > 0:
                            pl.w("scalar", sTE, sTE.n)
                        vq = pl.o("scalar", lambda e, s=xt: e.activation(
                            xq, s, AF.Square, bias=0.0, scale=1.0), inc=sSC)
                        pl.ws("tensor", [(sDV, vv), (sSC, vq)])
                        if kt == 0:
                            pl.ws("tensor", [pfree.get("pC0"), pfree.get("pC1")])
                        pl.o("tensor", lambda e, k=kt: e.matmul(
                            pC[0:1, 0:512], c_onec[:, 0:1], xr,
                            start=(k == 0), stop=(k == NT - 1)))
                        vmm = pl.o("tensor", lambda e, k=kt: e.matmul(
                            pC[0:1, 512:1024], c_onec[:, 0:1], xq,
                            start=(k == 0), stop=(k == NT - 1)), inc=sTE)
                    dep_rm = row_math_ln(pC[0:1, 0:512], pC[0:1, 512:1024],
                                         1, 2, dep=(sTE, vmm))
                    pfree["pC0"] = dep_rm
                    pfree["pC1"] = dep_rm
                    mrow, rrow = rslot(1), rslot(2)
                else:
                    mrow, rrow = rslot(m_base + sc), rslot(r_base + sc)
                    dep_rm = None
                dA, dB = bcast2(mrow, rrow, dep_rm)
                pl.w("sync", sDV, sDV.n)
                pl.w("sync", sSC, sSC.n)
                s_last = None
                for kt in range(NT):
                    xt = av(a3, 4096 + (kt % 2) * 2048, F32, 128, 512)
                    pl.w("sync", sDV, sDV.n)
                    lv = ld(xt, xin_d[kt, :, tsl])
                    stile = av(a2, 16384 + kt * 1024, BF16, 128, 512)
                    pl.ws("vector", [(sLD, lv), dA, dB])
                    if kt == 0:
                        pl.w("vector", sTE, sTE.n)
                    pl.o("vector", lambda e, x=xt: e.tensor_tensor(
                        t1v, x, pB[:, 0:512], ALU.subtract))
                    s_last = pl.o("vector", lambda e, d=stile, k=kt:
                                  e.scalar_tensor_tensor(
                                      d, t1v, b_su[:, k:k + 1],
                                      pB[:, 512:1024], ALU.mult, ALU.mult),
                                  inc=sDV)
                pfree["pB0"] = (sDV, s_last)
                pfree["pB1"] = (sDV, s_last)
                for ot in range(NT):
                    order = sorted(range(NT), key=lambda kt_: -H16S[ot, kt_])
                    pk = "p0" if ot % 2 == 0 else "p1"
                    pairs = []
                    for kt in order:
                        lt = c_hp if H16S[ot, kt] > 0 else c_hn
                        stile = av(a2, 16384 + kt * 1024, BF16, 128, 512)
                        pairs.append((lt[:, :], stile))
                    dep = mm_group(pk, pairs,
                                   waits=[(sDV, s_last)] if ot == 0 else [])
                    ftile = av(a2, ot * 1024, BF16, 128, 512)
                    pl.ws("vector", [dep])
                    vv = pl.o("vector", lambda e, d=ftile, s=PSV[pk]:
                              e.tensor_copy(d, s), inc=sDV)
                    pfree[pk] = (sDV, vv)
                panfree = {}
                castU = [0, 0]; stU = [0, 0]; stG1 = [0, 0]
                for kap in range(NKAP):
                    par = kap % 2
                    panU = wpanA if kap % 2 == 0 else wpanC
                    panG = wpanB[:, :] if kap % 2 == 0 else av(a3, 0, BF16, 128, 2048)
                    if ffn1 and sc == 0:
                        stgF = av(a3, 8192 + par * 8192, F32, 128, 2048)
                        pl.w("sync", sDV, castU[par])
                        lu = ld(stgF, fup_h[kap].rearrange("a p b -> p a b"))
                        pl.ws("vector", [(sLD, lu),
                                         (sTE, panfree.get(par, 0)),
                                         (sST, stU[par])])
                        castU[par] = pl.o("vector", lambda e, d=panU, s=stgF:
                                          e.tensor_copy(d[:, :], s), inc=sDV)
                        pl.w("sync", sDV, castU[par])
                        stU[par] = st(fupd[kap].rearrange("a p b -> p a b"),
                                      panU[:, :])
                        pl.w("gpsimd", sTE, panfree.get(par, 0))
                        pl.w("gpsimd", sST, stG1[par])
                        pl.w("gpsimd", sDV, s_last)
                        lg = ld(panG, fga_h[kap].rearrange("a p b -> p a b"),
                                q="gpsimd")
                        pl.w("sync", sWC, lg)
                        stG1[par] = st(fgad[kap].rearrange("a p b -> p a b"),
                                       panG)
                        wU = [(sDV, castU[par]), (sWC, lg)]
                    else:
                        pl.w("sync", sTE, panfree.get(kap % 2, 0))
                        lu = ld(panU[:, :], fupd[kap].rearrange(
                            "a p b -> p a b"))
                        lg = ld(panG[:, :], fgad[kap].rearrange(
                            "a p b -> p a b"))
                        wU = [(sLD, lg)]
                    pkU = "p0" if kap % 2 == 0 else "p2"
                    pkG = "p1" if kap % 2 == 0 else "p3"
                    pairsU, pairsG = [], []
                    for kt in range(NT):
                        ftile = av(a2, kt * 1024, BF16, 128, 512)
                        pairsU.append((panU[:, kt * 128:(kt + 1) * 128], ftile))
                        pairsG.append((panG[:, kt * 128:(kt + 1) * 128], ftile))
                    dU = mm_group(pkU, pairsU, waits=wU)
                    dG = mm_group(pkG, pairsG)
                    panfree[kap % 2] = dG[1]
                    pl.ws("scalar", [dG])
                    pl.w("scalar", sDV, sDV.n)
                    vs = pl.o("scalar", lambda e, s=PSV[pkG], k=kap:
                              e.activation(szb[:, :], s, AF.Silu, bias=0.0,
                                           scale=b_wsge[:, k:k + 1]), inc=sSC)
                    pfree[pkG] = (sSC, vs)
                    vbuf = av(a2, 16384 + (kap % 2) * 1024, BF16, 128, 512)
                    pl.ws("vector", [dU, (sSC, vs)])
                    if kap < 2:
                        pl.w("vector", sTE, sTE.n)
                    vv = pl.o("vector", lambda e, d=vbuf, u=PSV[pkU], k=kap:
                              e.scalar_tensor_tensor(
                                  d, u, b_wsue[:, k:k + 1], szb[:, :],
                                  ALU.mult, ALU.mult), inc=sDV)
                    pfree[pkU] = (sDV, vv)
                    pkX = "pB0" if kap % 2 == 0 else "pB1"
                    dX = mm_group(pkX, [(c_hd[:, :], vbuf)],
                                  waits=[(sDV, vv)])
                    gt = av(a1, kap * 1024, BF16, 128, 512)
                    pl.ws("vector", [dX])
                    vg = pl.o("vector", lambda e, d=gt, s=PSV[pkX]:
                              e.tensor_copy(d, s), inc=sDV)
                    pfree[pkX] = (sDV, vg)
                g_done = (sDV, sDV.n)
                if ffn1 and sc == 0:
                    pl.w("sync", sDV, max(castU))
                prev_group = 0
                for ot in range(NT):
                    pan0 = av(a3, 8192, BF16, 128, 32 * 128)
                    pan1 = av(a3, 16384, BF16, 128, 32 * 128)
                    pl.w("sync", sTE, prev_group)
                    lw0 = ld(pan0, wdhd[ot, 0:32].rearrange("a p b -> p a b"))
                    lw1 = ld(pan1, wdhd[ot, 32:64].rearrange("a p b -> p a b"))
                    pairs = []
                    for kap in range(64):
                        pan = pan0 if kap < 32 else pan1
                        k2 = kap % 32
                        gt = av(a1, kap * 1024, BF16, 128, 512)
                        pairs.append((pan[:, k2 * 128:(k2 + 1) * 128], gt))
                    w0 = [(sLD, lw0), g_done]
                    dD = mm_group("p3", pairs, waits=w0,
                                  mid_waits={32: [(sLD, lw1)]})
                    prev_group = dD[1]
                    xres = av(a3, 24576 + (ot % 2) * 2048, F32, 128, 512)
                    pl.w("sync", sDV, sDV.n)
                    lx = ld(xres, xin_d[ot, :, tsl])
                    stg = av(a3, 28672 + (ot % 2) * 2048, F32, 128, 512)
                    pl.ws("vector", [dD, (sLD, lx)])
                    pl.w("vector", sST, sST.n)
                    pl.w("vector", sSC, sSC.n)
                    vv = pl.o("vector", lambda e, d=stg, x=xres, o=ot:
                              e.scalar_tensor_tensor(
                                  d, p3[:, :], b_wsde[:, o:o + 1], x,
                                  ALU.mult, ALU.add), inc=sDV)
                    pfree["p3"] = (sDV, vv)
                    pl.w("sync", sDV, vv)
                    st(xout_d[ot, :, tsl], stg)
                    if ffn1:
                        pl.w("vector", sTE, sTE.n)
                        vr = pl.o("vector", lambda e, s=stg: e.tensor_copy(
                            xr, s), inc=sDV)
                        pl.w("scalar", sDV, vv)
                        pl.w("scalar", sTE, sTE.n)
                        vq = pl.o("scalar", lambda e, s=stg: e.activation(
                            xq, s, AF.Square, bias=0.0, scale=1.0), inc=sSC)
                        pl.ws("tensor", [(sDV, vr), (sSC, vq)])
                        if ot == 0:
                            pl.ws("tensor", [pfree.get("pC0"),
                                             pfree.get("pC1")])
                        pl.o("tensor", lambda e, o=ot: e.matmul(
                            pC[0:1, 0:512], c_onec[:, 0:1], xr,
                            start=(o == 0), stop=(o == NT - 1)))
                        vmm = pl.o("tensor", lambda e, o=ot: e.matmul(
                            pC[0:1, 512:1024], c_onec[:, 0:1], xq,
                            start=(o == 0), stop=(o == NT - 1)), inc=sTE)
                if ffn1:
                    dep_rm = row_math_ln(pC[0:1, 0:512], pC[0:1, 512:1024],
                                         4 + sc, 6 + sc, comb_slot=3,
                                         dep=(sTE, vmm))
                    pfree["pC0"] = dep_rm
                    pfree["pC1"] = dep_rm
                    dA, dB = bcast2(rslot(4 + sc), rslot(3), dep_rm)
                    for kt in range(NT):
                        xres = av(a3, 24576 + (kt % 2) * 2048, F32, 128, 512)
                        pl.w("sync", sDV, sDV.n)
                        lx = ld(xres, xout_d[kt, :, tsl])
                        ustg = av(a3, 32768 + (kt % 2) * 1024, BF16, 128, 512)
                        pl.w("vector", sST, sST.n)
                        pl.ws("vector", [(sLD, lx), dA, dB])
                        pl.o("vector", lambda e, x=xres: e.tensor_tensor(
                            t1v, x, pB[:, 0:512], ALU.subtract))
                        vv = pl.o("vector", lambda e, d=ustg, k=kt:
                                  e.scalar_tensor_tensor(
                                      d, t1v, b_n1[:, k:k + 1],
                                      pB[:, 512:1024], ALU.mult, ALU.mult),
                                  inc=sDV)
                        pl.w("sync", sDV, vv)
                        st(unin[kt, :, tsl], ustg)
                    pfree["pB0"] = (sDV, sDV.n)
                    pfree["pB1"] = (sDV, sDV.n)

        ffn(x0_h, x1d, True, None, None)
        pl.w("gpsimd", sST, sST.n)
        vag = pl.o("gpsimd", lambda e: e.collective_compute(
            "AllGather", ALU.bypass, replica_groups=RG,
            ins=[unin[:, :, :]], outs=[unout[:, :, :, :]]), inc=sCC)

        barrier()

        vmem = pl.o("vector", lambda e: e.memset(carry[:, :], 0.0), inc=sDV)
        abuf = av(a2, 16384, F32, 128, 1024)
        ubuf = av(a2, 20480, F32, 128, 1024)
        hcb = av(a2, 24576, F32, 128, 1024)
        dtx2v = av(a2, 28672, F32, 128, 1024)
        vcc_prev = 0
        gp_last = 0
        for ch in range(4):
            rk, cc_ = ch // 2, ch % 2
            csl = slice(cc_ * 512, (cc_ + 1) * 512)
            rhs_last = None
            for kt in range(NT):
                rt = av(a2, kt * 1024, BF16, 128, 512)
                pl.w("sync", sTE, sTE.n)
                pl.w("sync", sCC, vag)
                rhs_last = ld(rt, unout[rk, kt, :, csl])
            rhs_dep = (sLD, rhs_last)
            dtraw_dep = None
            zdep = None
            for ot in range(33):
                pan = wpanA if ot % 2 == 0 else wpanB
                pl.w("sync", sTE, sTE.n)
                lw = ld(pan[:, :], iwpd[ot].rearrange("a p b -> p a b"))
                pk = "p0" if ot % 2 == 0 else "p1"
                pairs = []
                for kt in range(NT):
                    rt = av(a2, kt * 1024, BF16, 128, 512)
                    pairs.append((pan[:, kt * 128:(kt + 1) * 128], rt))
                dmm = mm_group(pk, pairs, waits=[(sLD, lw), rhs_dep])
                if ot < 8:
                    dst = av(a3, ot * 2048, F32, 128, 512)
                elif ot < 16:
                    dst = av(a3, 16384 + (ot - 8) * 2048, F32, 128, 512)
                elif ot < 24:
                    dst = av(a3, 32768 + (ot - 16) * 2048, F32, 128, 512)
                elif ot == 24:
                    dst = av(a1, 32768, F32, 128, 512)
                else:
                    dst = None
                if dst is not None:
                    pl.ws("vector", [dmm])
                    vv = pl.o("vector", lambda e, d=dst, s=PSV[pk]:
                              e.tensor_copy(d, s), inc=sDV)
                    pfree[pk] = (sDV, vv)
                    if ot == 24:
                        db = av(a1, 34816, BF16, 128, 512)
                        vv2 = pl.o("vector", lambda e, d=db: e.tensor_copy(
                            d, av(a1, 32768, F32, 128, 512)), inc=sDV)
                        dtraw_dep = (sDV, vv2)
                else:
                    tau = ot - 25
                    zp = av(a1, 36864 + tau * 1024, BF16, 128, 512)
                    pl.ws("scalar", [dmm])
                    vs = pl.o("scalar", lambda e, d=zp, s=PSV[pk]:
                              e.activation(d, s, AF.Silu, bias=0.0,
                                           scale=1.0), inc=sSC)
                    pfree[pk] = (sSC, vs)
                    zdep = (sSC, vs)
            dt_deps = []
            for tau in range(8):
                pl.w("sync", sTE, sTE.n)
                lw = ld(wpanC[:, 0:128], dtpd[tau])
                db = av(a1, 34816, BF16, 128, 512)
                dmm = mm_group("p2", [(wpanC[:, 0:128], db)],
                               waits=[(sLD, lw), dtraw_dep])
                dtt = av(a1, tau * 2048, F32, 128, 512)
                pl.ws("scalar", [dmm])
                pl.w("scalar", sDV, sDV.n)
                pl.o("scalar", lambda e, d=dtt, s=PSV["p2"], t=tau:
                     e.activation(d, s, AF.Exp, bias=b_dtb[:, t:t + 1],
                                  scale=1.0))
                vs = pl.o("scalar", lambda e, d=dtt: e.activation(
                    d, d, AF.Ln, bias=1.0, scale=1.0), inc=sSC)
                pfree["p2"] = (sSC, vs)
                dt_deps.append((sSC, vs))
            last_bcast = 0
            for tau in range(8):
                dtt = av(a1, tau * 2048, F32, 128, 512)
                xrp = av(a3, 32768 + tau * 2048, F32, 128, 512)
                ytile = av(a1, 16384 + tau * 2048, F32, 128, 512)
                Ctf = av(a3, tau * 2048, F32, 128, 512)
                Btf = av(a3, 16384 + tau * 2048, F32, 128, 512)
                pl.w("vector", sTE, last_bcast)
                vstg = pl.o("vector", lambda e, s=Ctf: e.tensor_copy(ctr[:, :], s))
                vstg = pl.o("vector", lambda e, s=Btf: e.tensor_copy(btr[:, :], s),
                            inc=sDV)
                pl.ws("vector", [dt_deps[tau]])
                pl.o("vector", lambda e, d=dtt, x=xrp: e.tensor_tensor(
                    dtx2v[:, 0:512], d, x, ALU.mult))
                vdx = pl.o("vector", lambda e: e.tensor_copy(
                    dtx2v[:, 512:1024], dtx2v[:, 0:512]), inc=sDV)
                for np_ in range(16):
                    n0 = 2 * np_
                    ci = tau * 32 + n0
                    dBb = mm_group("pB0",
                                   [(c_eall[:, n0 * 128:(n0 + 1) * 128],
                                     btr[:, :])],
                                   waits=([(sDV, vdx), (sDV, vstg)]
                                          if np_ == 0 else []))
                    dBb2 = mm_group("pB1",
                                    [(c_eall[:, (n0 + 1) * 128:(n0 + 2) * 128],
                                      btr[:, :])])
                    dCb = mm_group("pC0",
                                   [(c_eall[:, n0 * 128:(n0 + 1) * 128],
                                     ctr[:, :])])
                    dCb2 = mm_group("pC1",
                                    [(c_eall[:, (n0 + 1) * 128:(n0 + 2) * 128],
                                      ctr[:, :])])
                    last_bcast = dCb2[1]
                    pl.w("scalar", sDV, sDV.n)
                    pl.o("scalar", lambda e, d=dtt, t=tau, n=n0: e.activation(
                        abuf[:, 0:512], d, AF.Exp, bias=0.0,
                        scale=b_acol[:, t * 32 + n:t * 32 + n + 1]))
                    va = pl.o("scalar", lambda e, d=dtt, t=tau, n=n0:
                              e.activation(abuf[:, 512:1024], d, AF.Exp,
                                           bias=0.0,
                                           scale=b_acol[:, t * 32 + n + 1:
                                                        t * 32 + n + 2]),
                              inc=sSC)
                    pl.ws("vector", [dBb, dBb2])
                    vu = pl.o("vector", lambda e: e.tensor_tensor(
                        ubuf[:, :], dtx2v[:, :], pB[:, :], ALU.mult), inc=sDV)
                    pfree["pB0"] = (sDV, vu)
                    pfree["pB1"] = (sDV, vu)
                    pl.w("vector", sSC, va)
                    pl.w("vector", sSC, vcc_prev)
                    pl.w("vector", sDV, vmem)
                    pl.o("vector", lambda e, c=ci: e.tensor_tensor_scan(
                        hbuf[:, 0:512], abuf[:, 0:512], ubuf[:, 0:512],
                        carry[:, c:c + 1], ALU.mult, ALU.add))
                    vh = pl.o("vector", lambda e, c=ci: e.tensor_tensor_scan(
                        hbuf[:, 512:1024], abuf[:, 512:1024],
                        ubuf[:, 512:1024], carry[:, c + 1:c + 2],
                        ALU.mult, ALU.add), inc=sDV)
                    pl.w("scalar", sDV, vh)
                    hsrc = bass_rust.AP(hbuf[:, :].tensor, 511,
                                        [[1024, 128], [512, 2]])
                    vcc_prev = pl.o("scalar", lambda e, c=ci, s=hsrc:
                                    e.activation(carry[:, c:c + 2], s, AF.Copy,
                                                 bias=0.0, scale=1.0), inc=sSC)
                    pl.ws("vector", [dCb, dCb2])
                    pl.w("vector", sGP, gp_last)
                    vhc = pl.o("vector", lambda e: e.tensor_tensor(
                        hcb[:, :], hbuf[:, :], pC[:, :], ALU.mult), inc=sDV)
                    pfree["pC0"] = (sDV, vhc)
                    pfree["pC1"] = (sDV, vhc)
                    pl.w("gpsimd", sDV, vhc)
                    if np_ == 0:
                        gp_last = pl.o("gpsimd", lambda e, y=ytile:
                                       e.tensor_tensor(y, hcb[:, 0:512],
                                                       hcb[:, 512:1024],
                                                       ALU.add), inc=sGP)
                    else:
                        pl.o("gpsimd", lambda e, y=ytile: e.tensor_tensor(
                            y, y, hcb[:, 0:512], ALU.add))
                        gp_last = pl.o("gpsimd", lambda e, y=ytile:
                                       e.tensor_tensor(y, y,
                                                       hcb[:, 512:1024],
                                                       ALU.add), inc=sGP)
                pl.w("vector", sGP, gp_last)
                vy = pl.o("vector", lambda e, y=ytile, x=xrp, t=tau:
                          e.scalar_tensor_tensor(y, x, b_dcol[:, t:t + 1],
                                                 y, ALU.mult, ALU.add),
                          inc=sDV)
                zp = av(a1, 36864 + tau * 1024, BF16, 128, 512)
                pl.w("scalar", sDV, vy)
                pl.ws("scalar", [zdep])
                vsz = pl.o("scalar", lambda e, z=zp: e.activation(
                    szb[:, :], z, AF.Copy, bias=0.0, scale=1.0), inc=sSC)
                v2t = av(a1, 49152 + tau * 1024, BF16, 128, 512)
                pl.w("vector", sSC, vsz)
                pl.o("vector", lambda e, d=v2t, y=ytile: e.tensor_tensor(
                    d, y, szb[:, :], ALU.mult), inc=sDV)
            v2_dep = (sDV, sDV.n)
            for ot in range(NT):
                pan = wpanA if ot % 2 == 0 else wpanB
                pl.w("sync", sTE, sTE.n)
                lw = ld(pan[:, 0:1024], opjd[ot].rearrange("a p b -> p a b"))
                pk = "p0" if ot % 2 == 0 else "p1"
                pairs = []
                for tau in range(8):
                    v2t = av(a1, 49152 + tau * 1024, BF16, 128, 512)
                    pairs.append((pan[:, tau * 128:(tau + 1) * 128], v2t))
                dmm = mm_group(pk, pairs, waits=[(sLD, lw), v2_dep])
                stg = av(a1, 57344 + (ot % 2) * 2048, F32, 128, 512)
                pl.ws("vector", [dmm])
                pl.w("vector", sST, sST.n)
                vv = pl.o("vector", lambda e, d=stg, s=PSV[pk]:
                          e.tensor_copy(d, s), inc=sDV)
                pfree[pk] = (sDV, vv)
                pl.w("sync", sDV, vv)
                st(rsin[rk, ot, :, csl], stg)
        pl.w("gpsimd", sST, sST.n)
        vrs = pl.o("gpsimd", lambda e: e.collective_compute(
            "ReduceScatter", ALU.add, replica_groups=RG,
            ins=[rsin[:, :, :, :]], outs=[rsout[:, :, :]]), inc=sCC)

        barrier()

        rs2dep = {}
        for sc in range(2):
            tsl = slice(sc * 512, (sc + 1) * 512)
            dA, dB = bcast2(rslot(4 + sc), rslot(6 + sc), (sCC, vrs))
            s2key = "pC1" if sc == 0 else "p2"
            s2row = pC[0:1, 512:1024] if sc == 0 else p2[0:1, 0:512]
            vmm = None
            for kt in range(NT):
                x1t = av(a2, 16384 + (kt % 2) * 2048, F32, 128, 512)
                mot = av(a2, 20480 + (kt % 2) * 2048, F32, 128, 512)
                hht = av(a2, 24576 + (kt % 2) * 2048, F32, 128, 512)
                pl.w("sync", sDV, sDV.n)
                pl.w("sync", sSC, sSC.n)
                ld(x1t, x1d[kt, :, tsl])
                pl.w("sync", sCC, vrs)
                lm = ld(mot, rsout[kt, :, tsl])
                pl.ws("vector", [(sLD, lm), dA, dB])
                if kt >= 2:
                    pl.w("vector", sSC, sSC.n)
                pl.o("vector", lambda e, x=x1t, h=hht: e.tensor_tensor(
                    h, x, pB[:, 0:512], ALU.subtract))
                pl.o("vector", lambda e, h=hht: e.scalar_tensor_tensor(
                    h, h, 1.0, pB[:, 512:1024], ALU.mult, ALU.mult))
                vv = pl.o("vector", lambda e, h=hht, m=mot: e.tensor_tensor(
                    h, h, m, ALU.add), inc=sDV)
                xqr = statr[:, 512:1024]
                pl.w("scalar", sDV, vv)
                pl.w("scalar", sTE, sTE.n)
                vq = pl.o("scalar", lambda e, h=hht: e.activation(
                    xqr, h, AF.Square, bias=0.0, scale=1.0), inc=sSC)
                pl.ws("tensor", [(sSC, vq)])
                if kt == 0:
                    pl.ws("tensor", [pfree.get(s2key)])
                vmm = pl.o("tensor", lambda e, k=kt, o=s2row: e.matmul(
                    o, c_onec[:, 0:1], xqr, start=(k == 0),
                    stop=(k == NT - 1)), inc=sTE)
            drm = row_math_ln(None, s2row, None, 3, dep=(sTE, vmm),
                              rms_only=True)
            pfree[s2key] = drm
            dR = mm_group("pC0", [(c_oner[0:1, :], rslot(3))], waits=[drm])
            for kt in range(NT):
                x1t = av(a2, 16384 + (kt % 2) * 2048, F32, 128, 512)
                mot = av(a2, 20480 + (kt % 2) * 2048, F32, 128, 512)
                hht = av(a2, 24576 + (kt % 2) * 2048, F32, 128, 512)
                pl.w("sync", sDV, sDV.n)
                pl.w("sync", sSC, sSC.n)
                ld(x1t, x1d[kt, :, tsl])
                lm = ld(mot, rsout[kt, :, tsl])
                xt2 = (av(a1, 45056 + kt * 1024, BF16, 128, 512) if sc == 0
                       else av(a2, kt * 1024, BF16, 128, 512))
                pl.ws("vector", [(sLD, lm), dA, dB, dR])
                pl.o("vector", lambda e, x=x1t, h=hht: e.tensor_tensor(
                    h, x, pB[:, 0:512], ALU.subtract))
                pl.o("vector", lambda e, h=hht: e.scalar_tensor_tensor(
                    h, h, 1.0, pB[:, 512:1024], ALU.mult, ALU.mult))
                pl.o("vector", lambda e, h=hht, m=mot: e.tensor_tensor(
                    h, h, m, ALU.add))
                vv = pl.o("vector", lambda e, d=xt2, h=hht, k=kt:
                          e.scalar_tensor_tensor(d, h, b_n2[:, k:k + 1],
                                                 pC[:, 0:512], ALU.mult,
                                                 ALU.mult), inc=sDV)
            rs2dep[sc] = (sDV, sDV.n)
            pfree["pC0"] = rs2dep[sc]
            pfree["pB0"] = rs2dep[sc]
            pfree["pB1"] = rs2dep[sc]
        panfree = {}
        stgF3 = av(a2, 16384, F32, 128, 2048)
        castU3 = 0
        castG3 = 0
        pre_v = rs2dep[1][1]
        for ka in range(NKA):
            panU = wpanA if ka % 2 == 0 else wpanC
            panG = wpanB[:, :] if ka % 2 == 0 else av(a3, 45056, BF16, 128, 2048)
            if ka == 0:
                pl.w("sync", sDV, pre_v)
            pl.w("sync", sDV, castU3)
            lu = ld(stgF3, auu_h[ka].rearrange("a p b -> p a b"))
            pl.ws("vector", [(sLD, lu), (sTE, panfree.get(ka % 2, 0))])
            castU3 = pl.o("vector", lambda e, d=panU, s=stgF3:
                          e.tensor_copy(d[:, :], s), inc=sDV)
            pl.w("gpsimd", sTE, panfree.get(ka % 2, 0))
            lg = ld(panG[:, :], agu_h[ka].rearrange("a p b -> p a b"),
                    q="gpsimd")
            for sc in range(2):
                aren = a1 if sc == 0 else a3
                pkU = "p0" if sc == 0 else "p2"
                pkG = "p1" if sc == 0 else "p3"
                pairsU, pairsG = [], []
                for kt in range(NT):
                    xt2 = (av(a1, 45056 + kt * 1024, BF16, 128, 512) if sc == 0
                           else av(a2, kt * 1024, BF16, 128, 512))
                    pairsU.append((panU[:, kt * 128:(kt + 1) * 128], xt2))
                    pairsG.append((panG[:, kt * 128:(kt + 1) * 128], xt2))
                dU = mm_group(pkU, pairsU, waits=[(sDV, castU3), (sWC, lg),
                                                  rs2dep[sc]])
                dG = mm_group(pkG, pairsG)
                if sc == 1:
                    panfree[ka % 2] = dG[1]
                pl.ws("scalar", [dG])
                pl.w("scalar", sDV, sDV.n)
                vs = pl.o("scalar", lambda e, s=PSV[pkG]: e.activation(
                    szb[:, :], s, AF.Silu, bias=0.0, scale=1.0), inc=sSC)
                pfree[pkG] = (sSC, vs)
                vat = av(aren, ka * 1024, BF16, 128, 512)
                pl.ws("vector", [dU, (sSC, vs)])
                vv = pl.o("vector", lambda e, d=vat, u=PSV[pkU]:
                          e.tensor_tensor(d, u, szb[:, :], ALU.mult), inc=sDV)
                pfree[pkU] = (sDV, vv)
        vat_dep = (sDV, sDV.n)
        pl.w("sync", sSC, castG3)
        vmm3 = {0: None, 1: None}
        prev_pan = {0: 0, 1: 0}
        for ot in range(NT):
            pidx = ot % 2
            pan = (av(a1, 45056, BF16, 128, 44 * 128) if pidx == 0
                   else av(a2, 0, BF16, 128, 44 * 128))
            pl.w("gpsimd", sTE, max(prev_pan[pidx], sTE.n if ot < 2 else 0))
            lw = ld(pan, adn_h[ot].rearrange("a p b -> p a b"), q="gpsimd")
            for sc in range(2):
                aren = a1 if sc == 0 else a3
                pk = "p0" if sc == 0 else "p1"
                tsl = slice(sc * 512, (sc + 1) * 512)
                pairs = []
                for ka in range(NKA):
                    vat = av(aren, ka * 1024, BF16, 128, 512)
                    pairs.append((pan[:, ka * 128:(ka + 1) * 128], vat))
                dmm = mm_group(pk, pairs, waits=[(sWC, lw), vat_dep])
                prev_pan[pidx] = dmm[1]
                dA, dB = bcast2(rslot(4 + sc), rslot(6 + sc), dmm)
                x1t = av(a2, 16384 + sc * 2048, F32, 128, 512)
                mot = av(a2, 20480 + sc * 2048, F32, 128, 512)
                pl.w("sync", sDV, sDV.n)
                ld(x1t, x1d[ot, :, tsl])
                lm = ld(mot, rsout[ot, :, tsl])
                hht = av(a2, 24576 + sc * 2048, F32, 128, 512)
                stg = av(a1, 61440 + sc * 2048, F32, 128, 512)
                pl.ws("vector", [(sLD, lm), dA, dB, dmm])
                pl.w("vector", sST, sST.n)
                pl.w("vector", sSC, sSC.n)
                pl.w("vector", sTE, sTE.n)
                pl.o("vector", lambda e, x=x1t, h=hht: e.tensor_tensor(
                    h, x, pB[:, 0:512], ALU.subtract))
                pl.o("vector", lambda e, h=hht: e.scalar_tensor_tensor(
                    h, h, 1.0, pB[:, 512:1024], ALU.mult, ALU.mult))
                pl.o("vector", lambda e, h=hht, x=x1t: e.tensor_tensor(
                    h, h, x, ALU.add))
                pl.o("vector", lambda e, h=hht, m=mot: e.tensor_tensor(
                    h, h, m, ALU.add))
                vv = pl.o("vector", lambda e, d=stg, h=hht, s=PSV[pk]:
                          e.scalar_tensor_tensor(d, s, 1.0, h, ALU.mult,
                                                 ALU.add), inc=sDV)
                pfree[pk] = (sDV, vv)
                pfree["pB0"] = (sDV, vv)
                pfree["pB1"] = (sDV, vv)
                pl.w("sync", sDV, vv)
                st(x2d[ot, :, tsl], stg)
                s1r = pC[0:1, 0:512] if sc == 0 else p2[0:1, 0:512]
                s2r = pC[0:1, 512:1024] if sc == 0 else p3[0:1, 0:512]
                k1 = "pC0" if sc == 0 else "p2"
                k2_ = "pC1" if sc == 0 else "p3"
                xr3 = statr[:, 0:512]
                xq3 = statr[:, 512:1024]
                pl.w("vector", sTE, sTE.n)
                vr = pl.o("vector", lambda e, s=stg, d=xr3: e.tensor_copy(
                    d, s), inc=sDV)
                pl.w("scalar", sDV, vv)
                pl.w("scalar", sTE, sTE.n)
                vq = pl.o("scalar", lambda e, s=stg, d=xq3: e.activation(
                    d, s, AF.Square, bias=0.0, scale=1.0), inc=sSC)
                pl.ws("tensor", [(sDV, vr), (sSC, vq)])
                if ot == 0:
                    pl.ws("tensor", [pfree.get(k1), pfree.get(k2_)])
                pl.o("tensor", lambda e, o=ot, r=s1r: e.matmul(
                    r, c_onec[:, 0:1], xr3, start=(o == 0),
                    stop=(o == NT - 1)))
                vmm3[sc] = pl.o("tensor", lambda e, o=ot, r=s2r: e.matmul(
                    r, c_onec[:, 0:1], xq3, start=(o == 0),
                    stop=(o == NT - 1)), inc=sTE)
        d30 = row_math_ln(pC[0:1, 0:512], pC[0:1, 512:1024], 4, 6,
                          dep=(sTE, vmm3[0]))
        pfree["pC0"] = d30
        pfree["pC1"] = d30
        d31 = row_math_ln(p2[0:1, 0:512], p3[0:1, 0:512], 5, 7,
                          dep=(sTE, vmm3[1]))
        pfree["p2"] = d31
        pfree["p3"] = d31

        barrier()

        ffn(x2d, out_h, False, 4, 6)

        pl.w("sync", sST, sST.n)
        pl.w("sync", sCC, sCC.n)

        with nc.Block() as block:
            @block.sync
            def _(e):
                for f in pl.q["sync"]:
                    f(e)

            @block.tensor
            def _(e):
                for f in pl.q["tensor"]:
                    f(e)

            @block.vector
            def _(e):
                for f in pl.q["vector"]:
                    f(e)

            @block.scalar
            def _(e):
                for f in pl.q["scalar"]:
                    f(e)

            @block.gpsimd
            def _(e):
                for f in pl.q["gpsimd"]:
                    f(e)
    return nc


def _prep_shared(kw):
    H128 = _had(128).astype(np.float32)
    H64 = _had(64).astype(np.float32)
    hp = H128.astype(ml_dtypes.bfloat16)
    hn = (-H128).astype(ml_dtypes.bfloat16)
    hd = np.zeros((128, 128), np.float32)
    hd[:64, :64] = H64
    hd[64:, 64:] = H64
    hd = hd.astype(ml_dtypes.bfloat16)
    eall = np.zeros((128, 32 * 128), np.float32)
    for n in range(32):
        for p in range(128):
            eall[(p // 32) * 32 + n, n * 128 + p] = 1.0
    onec = np.ones((128, 1), np.float32)
    oner = np.ones((1, 128), np.float32)

    perm = np.empty(8192, np.int64)
    for kap in range(64):
        for q in range(2):
            for jt in range(64):
                perm[kap * 128 + q * 64 + jt] = jt * 128 + 2 * kap + q
    wu = np.ascontiguousarray(np.asarray(kw["L0_ffn_W_up"])[perm, :])
    wg = np.ascontiguousarray(np.asarray(kw["L0_ffn_W_gate"])[perm, :])
    fup = np.ascontiguousarray(wu.reshape(64, 128, 16, 128).transpose(0, 2, 3, 1))
    fga = np.ascontiguousarray(wg.reshape(64, 128, 16, 128).transpose(0, 2, 3, 1))
    wdt = np.ascontiguousarray(np.asarray(kw["L0_ffn_W_down"]).T
                               .reshape(64, 128, 2048))

    agu = np.ascontiguousarray(np.asarray(kw["L0_attn_gate_w"])
                               .reshape(44, 128, 16, 128).transpose(0, 2, 3, 1))
    auu = np.ascontiguousarray(np.asarray(kw["L0_attn_up_w"])
                               .reshape(44, 128, 16, 128).transpose(0, 2, 3, 1))
    adn = np.ascontiguousarray(np.asarray(kw["L0_attn_down_w"]).T
                               .reshape(44, 128, 16, 128).transpose(2, 0, 1, 3))

    sucol = np.ascontiguousarray(np.asarray(kw["L0_ffn_SU_ug"]).reshape(16, 128).T)
    wsug = np.asarray(kw["L0_ffn_Wscale_ug"])
    wsu = np.ascontiguousarray(wsug[:8192][perm].reshape(64, 128).T)
    wsg = np.ascontiguousarray(wsug[8192:][perm].reshape(64, 128).T)
    sud = np.ascontiguousarray(np.asarray(kw["L0_ffn_SU_dp"])[perm]
                               .reshape(64, 128).T)
    wsd = np.ascontiguousarray(np.asarray(kw["L0_ffn_Wscale_dp"])
                               .reshape(16, 128).T)
    n1c = np.ascontiguousarray(np.asarray(kw["L0_attn_norm1_w"])
                               .reshape(16, 128).T)
    n2c = np.ascontiguousarray(np.asarray(kw["L0_attn_norm2_w"])
                               .reshape(16, 128).T)
    cmisc = np.tile(np.array([EPS, 1.0, -0.5, -1.0], np.float32), (128, 1))
    f32 = np.float32
    return dict(hp=hp, hn=hn, hd=hd, eall=eall.astype(f32), onec=onec,
                oner=oner, fup=fup.astype(f32), fga=fga.astype(f32),
                wdt=wdt.astype(f32), agu=agu.astype(f32), auu=auu.astype(f32),
                adn=adn.astype(f32), sucol=sucol.astype(f32),
                wsu=wsu.astype(f32), wsg=wsg.astype(f32), sud=sud.astype(f32),
                wsd=wsd.astype(f32), n1c=n1c.astype(f32), n2c=n2c.astype(f32),
                cmisc=cmisc)


def _prep_half(kw, hk):
    W = np.asarray(kw["L0_attn_in_proj_w"])
    rows = []
    for tau in range(8):
        rows.extend(range(3072 + 1024 * hk + tau * 128,
                          3072 + 1024 * hk + tau * 128 + 128))
    for tau in range(8):
        for k in range(128):
            rows.append(2560 + (8 * hk + tau) * 32 + (k % 32))
    for tau in range(8):
        for k in range(128):
            rows.append(2048 + (8 * hk + tau) * 32 + (k % 32))
    rows.extend(range(5120, 5248))
    rows.extend(range(1024 * hk, 1024 * hk + 1024))
    Ws = W[np.array(rows), :]
    iwp = np.ascontiguousarray(Ws.reshape(33, 128, 16, 128).transpose(0, 2, 3, 1))

    dpw = np.asarray(kw["L0_attn_dt_proj_w"])[1024 * hk:1024 * hk + 1024]
    dtp = np.ascontiguousarray(dpw.reshape(8, 128, 128).transpose(0, 2, 1))
    dtb = np.ascontiguousarray(np.asarray(kw["L0_attn_dt_proj_b"])
                               [1024 * hk:1024 * hk + 1024].reshape(8, 128).T)
    alog = np.ascontiguousarray(np.asarray(kw["L0_attn_A_log"])
                                [1024 * hk:1024 * hk + 1024]
                                .reshape(8, 128, 32).transpose(1, 0, 2)
                                .reshape(128, 256))
    dcol = np.ascontiguousarray(np.asarray(kw["L0_attn_D_param"])
                                [1024 * hk:1024 * hk + 1024].reshape(8, 128).T)
    opw = np.asarray(kw["L0_attn_out_proj_w"])[:, 1024 * hk:1024 * hk + 1024]
    opj = np.ascontiguousarray(opw.T.reshape(8, 128, 16, 128)
                               .transpose(2, 0, 1, 3))
    f32 = np.float32
    return dict(iwp=iwp.astype(f32), dtp=dtp.astype(f32), dtb=dtb.astype(f32),
                alog=alog.astype(f32), dcol=dcol.astype(f32),
                opj=opj.astype(f32))


def kernel(**kw):
    shared = _prep_shared(kw)
    halves = [_prep_half(kw, 0), _prep_half(kw, 1)]
    x = np.asarray(kw["L0_attn_hidden_states"])
    nc = build_nc()
    in_maps = []
    for k in range(8):
        bk, hk = k // 2, k % 2
        x0 = np.ascontiguousarray(
            x[bk, 1024 * hk:1024 * (hk + 1), :].T).reshape(16, 128, 1024)
        m = dict(x0=np.ascontiguousarray(x0, dtype=np.float32))
        m.update(shared)
        m.update(halves[hk])
        in_maps.append(m)
    res = run_bass_kernel_spmd(nc, in_maps, core_ids=list(range(8)))
    out = np.zeros((4, 2048, 2048), np.float32)
    for k in range(8):
        bk, hk = k // 2, k % 2
        o = np.asarray(res.results[k]["out"]).reshape(2048, 1024)
        out[bk, 1024 * hk:1024 * (hk + 1), :] = o.T
    return out


# version 10
# speedup vs baseline: 1.0756x; 1.0756x over previous
import numpy as np
import ml_dtypes
import concourse.bass as bass
import concourse.mybir as mybir
import bass_rust
from concourse.bass_utils import run_bass_kernel_spmd
from contextlib import ExitStack

F32 = mybir.dt.float32
F32R = mybir.dt.float32r
BF16 = mybir.dt.bfloat16
U8 = mybir.dt.uint8
AF = mybir.ActivationFunctionType
ALU = mybir.AluOpType

D = 2048
TL = 1024
NT = 16
NKA = 44
NKAP = 64
EPS = 1e-5
INV_D = 1.0 / 2048.0
RG = [[0, 1], [2, 3], [4, 5], [6, 7]]
ENGS = ("sync", "tensor", "vector", "scalar", "gpsimd")


def _had(n):
    h = np.array([[1.0]], dtype=np.float64)
    while h.shape[0] < n:
        h = np.block([[h, h], [h, -h]])
    return h


class Sem:
    __slots__ = ("h", "n")

    def __init__(self):
        self.h = None
        self.n = 0


class Plan:
    def __init__(self):
        self.q = {e: [] for e in ENGS}

    def w(self, eng, sem, val):
        if val and val > 0:
            self.q[eng].append(lambda e, s=sem, v=val: e.wait_ge(s.h, v))

    def ws(self, eng, deps):
        for d in deps:
            if d is not None:
                self.w(eng, d[0], d[1])

    def o(self, eng, fn, inc=None, k=1):
        if inc is None:
            self.q[eng].append(fn)
            return None
        inc.n += k
        val = inc.n
        self.q[eng].append(lambda e, f=fn, s=inc, kk=k: f(e).then_inc(s.h, kk))
        return val


def build_nc():
    nc = bass.Bass("TRN2", target_bir_lowering=False, debug=False)

    def din(name, shape, dt=F32):
        return nc.dram_tensor(name, list(shape), dt, kind="ExternalInput").ap()

    def dint(name, shape, dt=F32):
        return nc.dram_tensor(name, list(shape), dt, kind="Internal").ap()

    x0_h = din("x0", [NT, 128, TL])
    iwp_h = din("iwp", [33, 16, 128, 128])
    dtp_h = din("dtp", [8, 128, 128])
    dtb_h = din("dtb", [128, 8])
    alog_h = din("alog", [128, 256])
    dcol_h = din("dcol", [128, 8])
    opj_h = din("opj", [16, 8, 128, 128])
    fup_h = din("fup", [64, 16, 128, 128])
    fga_h = din("fga", [64, 16, 128, 128])
    wdt_h = din("wdt", [64, 128, D])
    agu_h = din("agu", [44, 16, 128, 128])
    auu_h = din("auu", [44, 16, 128, 128])
    adn_h = din("adn", [16, 44, 128, 128])
    hp_h = din("hp", [128, 128], BF16)
    hn_h = din("hn", [128, 128], BF16)
    hd_h = din("hd", [128, 128], BF16)
    eall_h = din("eall", [128, 32 * 128], F32R)
    onec_h = din("onec", [128, 1], F32R)
    oner_h = din("oner", [1, 128], F32R)
    sucol_h = din("sucol", [128, 16])
    wsu_h = din("wsu", [128, 64])
    wsg_h = din("wsg", [128, 64])
    sud_h = din("sud", [128, 64])
    wsd_h = din("wsd", [128, 16])
    n1c_h = din("n1c", [128, 16])
    n2c_h = din("n2c", [128, 16])
    cmisc_h = din("cmisc", [128, 4])

    out_h = nc.dram_tensor("out", [NT, 128, TL], F32, kind="ExternalOutput").ap()

    x1d = dint("x1d", [NT, 128, TL])
    x2d = dint("x2d", [NT, 128, TL])
    unin = dint("unin", [NT, 128, TL], BF16)
    unout = dint("unout", [2, NT, 128, TL], BF16)
    rsin = dint("rsin", [2, NT, 128, TL])
    rsout = dint("rsout", [NT, 128, TL])
    wdhd = dint("wdhd", [16, 64, 128, 128], BF16)
    iwpd = dint("iwpd", [33, 16, 128, 128], BF16)
    dtpd = dint("dtpd", [8, 128, 128], BF16)
    opjd = dint("opjd", [16, 8, 128, 128], BF16)
    fupd = dint("fupd", [64, 16, 128, 128], BF16)
    fgad = dint("fgad", [64, 16, 128, 128], BF16)

    es = ExitStack()
    with es:
        def S(name, shape, dt=F32):
            return es.enter_context(nc.sbuf_tensor(name, list(shape), dt))

        c_hp = S("c_hp", [128, 128], BF16)
        c_hn = S("c_hn", [128, 128], BF16)
        c_hd = S("c_hd", [128, 128], BF16)
        c_eall = S("c_eall", [128, 4096], F32R)
        c_onec = S("c_onec", [128, 1], F32R)
        c_oner = S("c_oner", [1, 128], F32R)
        b_dtb = S("b_dtb", [128, 8])
        b_acol = S("b_acol", [128, 256])
        b_dcol = S("b_dcol", [128, 8])
        b_su = S("b_su", [128, 16])
        b_wsu = S("b_wsu", [128, 64])
        b_wsg = S("b_wsg", [128, 64])
        b_sud = S("b_sud", [128, 64])
        b_wsd = S("b_wsd", [128, 16])
        b_n1 = S("b_n1", [128, 16])
        b_n2 = S("b_n2", [128, 16])
        b_wsue = S("b_wsue", [128, 64])
        b_wsge = S("b_wsge", [128, 64])
        b_wsde = S("b_wsde", [128, 16])
        c_misc = S("c_misc", [128, 4])
        rows = S("rows", [1, 4096], F32R)
        hbuf = S("hbuf", [128, 1024])
        carry = S("carry", [128, 256])
        szb = S("szb", [128, 512])
        wpanA = S("wpanA", [128, 2048], BF16)
        wpanB = S("wpanB", [128, 2048], BF16)
        wpanC = S("wpanC", [128, 2048], BF16)
        statr = S("statr", [128, 1024], F32R)
        ctr = S("ctr", [128, 512], F32R)
        btr = S("btr", [128, 512], F32R)
        a1 = S("a1", [128, 65536], U8)
        a2 = S("a2", [128, 32768], U8)
        a3 = S("a3", [128, 49152], U8)

        p0 = es.enter_context(nc.psum_tensor("p0", [128, 512], F32))
        p1 = es.enter_context(nc.psum_tensor("p1", [128, 512], F32))
        p2 = es.enter_context(nc.psum_tensor("p2", [128, 512], F32))
        p3 = es.enter_context(nc.psum_tensor("p3", [128, 512], F32))
        pB = es.enter_context(nc.psum_tensor("pB", [128, 1024], F32))
        pC = es.enter_context(nc.psum_tensor("pC", [128, 1024], F32))
        PSV = {"p0": p0[:, :], "p1": p1[:, :], "p2": p2[:, :], "p3": p3[:, :],
               "pB0": pB[:, 0:512], "pB1": pB[:, 512:1024],
               "pC0": pC[:, 0:512], "pC1": pC[:, 512:1024]}

        sems = {n: Sem() for n in ("sLD", "sST", "sWC", "sCC", "sTE", "sDV",
                                   "sSC", "sGP", "sS2", "sT2")}
        for n, s in sems.items():
            s.h = es.enter_context(nc.semaphore(n))
        sLD = sems["sLD"]; sST = sems["sST"]; sWC = sems["sWC"]
        sCC = sems["sCC"]; sTE = sems["sTE"]; sDV = sems["sDV"]
        sSC = sems["sSC"]; sGP = sems["sGP"]
        sS2 = sems["sS2"]; sT2 = sems["sT2"]

        nc.const_aps.register(F32, EPS, c_misc[:, 0:1])
        nc.const_aps.register(F32, 1.0, c_misc[:, 1:2])
        nc.const_aps.register(F32, -0.5, c_misc[:, 2:3])
        nc.const_aps.register(F32, -1.0, c_misc[:, 3:4])

        pl = Plan()
        pfree = {}

        def av(arena, off_bytes, dt, rows_, cols):
            bs = {F32: 4, F32R: 4, BF16: 2, U8: 1}[dt]
            return arena[0:rows_, off_bytes:off_bytes + cols * bs].bitcast(dt)

        def ld(dst, src, q="sync"):
            sem = {"sync": sLD, "gpsimd": sWC, "scalar": sS2}[q]
            return pl.o(q, lambda e, d=dst, s=src: e.dma_start(out=d, in_=s),
                        inc=sem, k=16)

        def st(dst, src, q="sync"):
            sem = sST if q == "sync" else sT2
            return pl.o(q, lambda e, d=dst, s=src: e.dma_start(out=d, in_=s),
                        inc=sem, k=16)

        def mm_group(pk, pairs, waits=(), mid_waits=None, out=None):
            ps = out if out is not None else PSV[pk]
            deps = list(waits)
            if pk in pfree:
                deps.append(pfree[pk])
            n = len(pairs)
            v = None
            for i, (lt, rh) in enumerate(pairs):
                if i == 0:
                    pl.ws("tensor", deps)
                if mid_waits and i in mid_waits:
                    pl.ws("tensor", mid_waits[i])
                if i == n - 1:
                    v = pl.o("tensor", lambda e, o=ps, l=lt, r=rh, s=(i == 0):
                             e.matmul(o, l, r, start=s, stop=True), inc=sTE)
                else:
                    pl.o("tensor", lambda e, o=ps, l=lt, r=rh, s=(i == 0):
                         e.matmul(o, l, r, start=s, stop=False))
            return (sTE, v)

        def barrier():
            vals = [(s, s.n) for s in sems.values()]
            for eng in ENGS:
                for (s, v) in vals:
                    pl.w(eng, s, v)

        def rslot(i):
            return rows[0:1, i * 512:(i + 1) * 512]

        def row_math_ln(s1_row, s2_row, m_slot, r_slot, comb_slot=None,
                        dep=None, rms_only=False):
            m = rslot(m_slot) if m_slot is not None else None
            r = rslot(r_slot)
            sc3 = rslot(0)
            pl.ws("vector", [dep] if dep else [])
            if rms_only:
                vv = pl.o("vector", lambda e: e.tensor_scalar(
                    r, s2_row, INV_D, None, ALU.mult), inc=sDV)
                pl.w("scalar", sDV, vv)
                pl.o("scalar", lambda e: e.activation(sc3, r, AF.Ln,
                                                      bias=EPS, scale=1.0))
                vs = pl.o("scalar", lambda e: e.activation(
                    r, sc3, AF.Exp, bias=0.0, scale=-0.5), inc=sSC)
                return (sSC, vs)
            pl.o("vector", lambda e: e.tensor_scalar(m, s1_row, INV_D, None,
                                                     ALU.mult))
            pl.o("vector", lambda e: e.tensor_tensor(sc3, m, m, ALU.mult))
            vv = pl.o("vector", lambda e: e.scalar_tensor_tensor(
                r, s2_row, INV_D, sc3, ALU.mult, ALU.subtract), inc=sDV)
            pl.w("scalar", sDV, vv)
            pl.o("scalar", lambda e: e.activation(sc3, r, AF.Ln, bias=EPS,
                                                  scale=1.0))
            if comb_slot is None:
                vs = pl.o("scalar", lambda e: e.activation(
                    r, sc3, AF.Exp, bias=0.0, scale=-0.5), inc=sSC)
                return (sSC, vs)
            cb = rslot(comb_slot)
            pl.o("scalar", lambda e: e.activation(r, sc3, AF.Exp, bias=0.0,
                                                  scale=-0.5))
            vs = pl.o("scalar", lambda e: e.activation(cb, sc3, AF.Exp,
                                                       bias=0.0, scale=-1.0),
                      inc=sSC)
            pl.w("vector", sSC, vs)
            pl.o("vector", lambda e: e.tensor_tensor(sc3, m, m, ALU.mult))
            pl.o("vector", lambda e: e.scalar_tensor_tensor(
                sc3, s2_row, INV_D, sc3, ALU.mult, ALU.subtract))
            vv2 = pl.o("vector", lambda e: e.tensor_tensor(cb, sc3, cb,
                                                           ALU.mult), inc=sDV)
            pl.w("scalar", sDV, vv2)
            pl.o("scalar", lambda e: e.activation(cb, cb, AF.Ln, bias=EPS,
                                                  scale=1.0))
            vs2 = pl.o("scalar", lambda e: e.activation(cb, cb, AF.Exp,
                                                        bias=0.0, scale=-0.5),
                       inc=sSC)
            pl.w("vector", sSC, vs2)
            vv3 = pl.o("vector", lambda e: e.tensor_tensor(cb, r, cb,
                                                           ALU.mult), inc=sDV)
            return (sDV, vv3)

        def bcast2(rowA, rowB, dep):
            dA = mm_group("pB0", [(c_oner[0:1, :], rowA)], waits=[dep])
            dB = mm_group("pB1", [(c_oner[0:1, :], rowB)], waits=[dep])
            return dA, dB

        for dst, src in [(c_hp, hp_h), (c_hn, hn_h), (c_hd, hd_h),
                         (c_eall, eall_h), (c_onec, onec_h), (c_oner, oner_h),
                         (b_dtb, dtb_h), (b_acol, alog_h), (b_dcol, dcol_h),
                         (b_su, sucol_h), (b_wsu, wsu_h), (b_wsg, wsg_h),
                         (b_sud, sud_h), (b_wsd, wsd_h), (b_n1, n1c_h),
                         (b_n2, n2c_h), (c_misc, cmisc_h)]:
            ld(dst[:, :], src[:, :])
        v_cl = sLD.n
        pl.w("scalar", sLD, v_cl)
        pl.o("scalar", lambda e: e.activation(b_acol[:, :], b_acol[:, :],
                                              AF.Exp, bias=0.0, scale=1.0))
        pl.o("scalar", lambda e: e.mul(b_acol[:, :], b_acol[:, :], -1.0),
             inc=sSC)
        pl.w("vector", sLD, v_cl)
        pl.o("vector", lambda e: e.tensor_scalar(
            b_wsge[:, :], b_wsg[:, :], float(1.0 / np.sqrt(2048.0)), None,
            ALU.mult))
        pl.o("vector", lambda e: e.scalar_tensor_tensor(
            b_wsue[:, :], b_wsu[:, :], float(1.0 / np.sqrt(2048.0)),
            b_sud[:, :], ALU.mult, ALU.mult))
        pl.o("vector", lambda e: e.tensor_scalar(
            b_wsde[:, :], b_wsd[:, :], float(0.5 / np.sqrt(8192.0)), None,
            ALU.mult), inc=sDV)

        pS_items = [(iwp_h[i].rearrange("a p b -> p a b"),
                     iwpd[i].rearrange("a p b -> p a b"), 2048)
                    for i in range(33)]
        pG_items = ([(opj_h[i].rearrange("a p b -> p a b"),
                      opjd[i].rearrange("a p b -> p a b"), 1024)
                     for i in range(16)]
                    + [(dtp_h[i], dtpd[i], 128) for i in range(8)])
        castS = [0, 0]; stS = [0, 0]; stG = [0, 0]
        mmW = [0, 0]; stW = [0, 0]
        si = gi = 0
        for i1 in range(64):
            par = i1 % 2
            pl.w("gpsimd", sTE, mmW[par])
            lv1 = ld(wpanA[:, par * 1024:(par + 1) * 1024],
                     wdt_h[i1, :, 0:1024], q="gpsimd")
            lv2 = ld(wpanB[:, par * 1024:(par + 1) * 1024],
                     wdt_h[i1, :, 1024:2048], q="gpsimd")
            evd = []
            for half, pan in ((0, wpanA), (1, wpanB)):
                for oc in range(2):
                    pk = "p0" if (half * 2 + oc) % 2 == 0 else "p1"
                    dep = mm_group(pk, [(c_hp[:, :].bitcast(BF16),
                                         pan[:, par * 1024 + oc * 512:
                                             par * 1024 + (oc + 1) * 512])],
                                   waits=[(sWC, lv2), (sLD, v_cl),
                                          (sST, stW[par])])
                    dst = av(a1, par * 4096 + (half * 1024 + oc * 512) * 2,
                             BF16, 128, 512)
                    pl.ws("vector", [dep])
                    vv = pl.o("vector", lambda e, d=dst, s=PSV[pk]:
                              e.tensor_copy(d, s), inc=sDV)
                    pfree[pk] = (sDV, vv)
                    evd.append(vv)
            mmW[par] = dep[1]
            stage3 = av(a1, par * 4096, BF16, 128, 2048).rearrange(
                "p (a b) -> p a b", a=16, b=128)
            dstap = bass_rust.AP(wdhd.tensor, i1 * 128,
                                 [[8192, 128], [64 * 128 * 128, 16], [1, 128]])
            pl.w("sync", sDV, evd[-1])
            stW[par] = st(dstap, stage3)
            if si < len(pS_items):
                j = si % 2
                srcS, dstS, colS = pS_items[si]
                stgF = av(a3, j * 8192, F32, 128, colS)
                stgB = av(a2, j * 4096, BF16, 128, colS)
                pl.w("sync", sSC, castS[j])
                lvS = ld(stgF, srcS)
                pl.ws("scalar", [(sLD, lvS), (sST, stS[j])])
                castS[j] = pl.o("scalar", lambda e, d=stgB, s=stgF:
                                e.activation(d, s, AF.Copy, bias=0.0,
                                             scale=1.0), inc=sSC)
                pl.w("sync", sSC, castS[j])
                stS[j] = st(dstS, stgB)
                si += 1
            if i1 % 2 == 1 and gi < len(pG_items):
                j = gi % 2
                srcG, dstG, colG = pG_items[gi]
                stgG = av(a2, 8192 + j * 4096, BF16, 128, colG)
                pl.w("gpsimd", sST, stG[j])
                lvG = ld(stgG, srcG, q="gpsimd")
                pl.w("sync", sWC, lvG)
                stG[j] = st(dstG, stgG)
                gi += 1
        v_wdh = sST.n

        barrier()

        H16S = np.sign(_had(16)).astype(int)

        def ffn(xin_d, xout_d, ffn1, m_base, r_base):
            t1v = av(a3, 2048, F32, 128, 512)
            for sc in range(2):
                tsl = slice(sc * 512, (sc + 1) * 512)
                xr = statr[:, 0:512]
                xq = statr[:, 512:1024]
                if ffn1:
                    for kt in range(NT):
                        xt = av(a3, 4096 + (kt % 2) * 2048, F32, 128, 512)
                        pl.w("sync", sDV, sDV.n)
                        pl.w("sync", sSC, sSC.n)
                        lv = ld(xt, xin_d[kt, :, tsl])
                        pl.w("vector", sLD, lv)
                        if kt > 0:
                            pl.w("vector", sTE, sTE.n)
                        vv = pl.o("vector", lambda e, s=xt: e.tensor_copy(xr, s),
                                  inc=sDV)
                        pl.w("scalar", sLD, lv)
                        if kt > 0:
                            pl.w("scalar", sTE, sTE.n)
                        vq = pl.o("scalar", lambda e, s=xt: e.activation(
                            xq, s, AF.Square, bias=0.0, scale=1.0), inc=sSC)
                        pl.ws("tensor", [(sDV, vv), (sSC, vq)])
                        if kt == 0:
                            pl.ws("tensor", [pfree.get("pC0"), pfree.get("pC1")])
                        pl.o("tensor", lambda e, k=kt: e.matmul(
                            pC[0:1, 0:512], c_onec[:, 0:1], xr,
                            start=(k == 0), stop=(k == NT - 1)))
                        vmm = pl.o("tensor", lambda e, k=kt: e.matmul(
                            pC[0:1, 512:1024], c_onec[:, 0:1], xq,
                            start=(k == 0), stop=(k == NT - 1)), inc=sTE)
                    dep_rm = row_math_ln(pC[0:1, 0:512], pC[0:1, 512:1024],
                                         1, 2, dep=(sTE, vmm))
                    pfree["pC0"] = dep_rm
                    pfree["pC1"] = dep_rm
                    mrow, rrow = rslot(1), rslot(2)
                else:
                    mrow, rrow = rslot(m_base + sc), rslot(r_base + sc)
                    dep_rm = None
                dA, dB = bcast2(mrow, rrow, dep_rm)
                pl.w("sync", sDV, sDV.n)
                pl.w("sync", sSC, sSC.n)
                s_last = None
                for kt in range(NT):
                    xt = av(a3, 4096 + (kt % 2) * 2048, F32, 128, 512)
                    pl.w("sync", sDV, sDV.n)
                    lv = ld(xt, xin_d[kt, :, tsl])
                    stile = av(a2, 16384 + kt * 1024, BF16, 128, 512)
                    pl.ws("vector", [(sLD, lv), dA, dB])
                    if kt == 0:
                        pl.w("vector", sTE, sTE.n)
                    pl.o("vector", lambda e, x=xt: e.tensor_tensor(
                        t1v, x, pB[:, 0:512], ALU.subtract))
                    s_last = pl.o("vector", lambda e, d=stile, k=kt:
                                  e.scalar_tensor_tensor(
                                      d, t1v, b_su[:, k:k + 1],
                                      pB[:, 512:1024], ALU.mult, ALU.mult),
                                  inc=sDV)
                pfree["pB0"] = (sDV, s_last)
                pfree["pB1"] = (sDV, s_last)
                for ot in range(NT):
                    order = sorted(range(NT), key=lambda kt_: -H16S[ot, kt_])
                    pk = "p0" if ot % 2 == 0 else "p1"
                    pairs = []
                    for kt in order:
                        lt = c_hp if H16S[ot, kt] > 0 else c_hn
                        stile = av(a2, 16384 + kt * 1024, BF16, 128, 512)
                        pairs.append((lt[:, :], stile))
                    dep = mm_group(pk, pairs,
                                   waits=[(sDV, s_last)] if ot == 0 else [])
                    ftile = av(a2, ot * 1024, BF16, 128, 512)
                    pl.ws("vector", [dep])
                    vv = pl.o("vector", lambda e, d=ftile, s=PSV[pk]:
                              e.tensor_copy(d, s), inc=sDV)
                    pfree[pk] = (sDV, vv)
                panfree = {}
                castU = [0, 0]; stU = [0, 0]; stG1 = [0, 0]
                for kap in range(NKAP):
                    par = kap % 2
                    panU = wpanA if kap % 2 == 0 else wpanC
                    panG = wpanB[:, :] if kap % 2 == 0 else av(a3, 0, BF16, 128, 2048)
                    if ffn1 and sc == 0:
                        stgF = av(a3, 8192 + par * 8192, F32, 128, 2048)
                        pl.w("sync", sDV, castU[par])
                        lu = ld(stgF, fup_h[kap].rearrange("a p b -> p a b"))
                        pl.ws("vector", [(sLD, lu),
                                         (sTE, panfree.get(par, 0)),
                                         (sST, stU[par])])
                        castU[par] = pl.o("vector", lambda e, d=panU, s=stgF:
                                          e.tensor_copy(d[:, :], s), inc=sDV)
                        pl.w("sync", sDV, castU[par])
                        stU[par] = st(fupd[kap].rearrange("a p b -> p a b"),
                                      panU[:, :])
                        pl.w("gpsimd", sTE, panfree.get(par, 0))
                        pl.w("gpsimd", sST, stG1[par])
                        pl.w("gpsimd", sDV, s_last)
                        lg = ld(panG, fga_h[kap].rearrange("a p b -> p a b"),
                                q="gpsimd")
                        pl.w("sync", sWC, lg)
                        stG1[par] = st(fgad[kap].rearrange("a p b -> p a b"),
                                       panG)
                        wU = [(sDV, castU[par]), (sWC, lg)]
                    else:
                        pl.w("sync", sTE, panfree.get(kap % 2, 0))
                        lu = ld(panU[:, :], fupd[kap].rearrange(
                            "a p b -> p a b"))
                        lg = ld(panG[:, :], fgad[kap].rearrange(
                            "a p b -> p a b"))
                        wU = [(sLD, lg)]
                    pkU = "p0" if kap % 2 == 0 else "p2"
                    pkG = "p1" if kap % 2 == 0 else "p3"
                    pairsU, pairsG = [], []
                    for kt in range(NT):
                        ftile = av(a2, kt * 1024, BF16, 128, 512)
                        pairsU.append((panU[:, kt * 128:(kt + 1) * 128], ftile))
                        pairsG.append((panG[:, kt * 128:(kt + 1) * 128], ftile))
                    dU = mm_group(pkU, pairsU, waits=wU)
                    dG = mm_group(pkG, pairsG)
                    panfree[kap % 2] = dG[1]
                    pl.ws("scalar", [dG])
                    pl.w("scalar", sDV, sDV.n)
                    vs = pl.o("scalar", lambda e, s=PSV[pkG], k=kap:
                              e.activation(szb[:, :], s, AF.Silu, bias=0.0,
                                           scale=b_wsge[:, k:k + 1]), inc=sSC)
                    pfree[pkG] = (sSC, vs)
                    vbuf = av(a2, 16384 + (kap % 2) * 1024, BF16, 128, 512)
                    pl.ws("vector", [dU, (sSC, vs)])
                    if kap < 2:
                        pl.w("vector", sTE, sTE.n)
                    vv = pl.o("vector", lambda e, d=vbuf, u=PSV[pkU], k=kap:
                              e.scalar_tensor_tensor(
                                  d, u, b_wsue[:, k:k + 1], szb[:, :],
                                  ALU.mult, ALU.mult), inc=sDV)
                    pfree[pkU] = (sDV, vv)
                    pkX = "pB0" if kap % 2 == 0 else "pB1"
                    dX = mm_group(pkX, [(c_hd[:, :], vbuf)],
                                  waits=[(sDV, vv)])
                    gt = av(a1, kap * 1024, BF16, 128, 512)
                    pl.ws("vector", [dX])
                    vg = pl.o("vector", lambda e, d=gt, s=PSV[pkX]:
                              e.tensor_copy(d, s), inc=sDV)
                    pfree[pkX] = (sDV, vg)
                g_done = (sDV, sDV.n)
                if ffn1 and sc == 0:
                    pl.w("sync", sDV, max(castU))
                prev_group = 0
                for ot in range(NT):
                    pan0 = av(a3, 8192, BF16, 128, 32 * 128)
                    pan1 = av(a3, 16384, BF16, 128, 32 * 128)
                    pl.w("sync", sTE, prev_group)
                    lw0 = ld(pan0, wdhd[ot, 0:32].rearrange("a p b -> p a b"))
                    lw1 = ld(pan1, wdhd[ot, 32:64].rearrange("a p b -> p a b"))
                    pairs = []
                    for kap in range(64):
                        pan = pan0 if kap < 32 else pan1
                        k2 = kap % 32
                        gt = av(a1, kap * 1024, BF16, 128, 512)
                        pairs.append((pan[:, k2 * 128:(k2 + 1) * 128], gt))
                    w0 = [(sLD, lw0), g_done]
                    dD = mm_group("p3", pairs, waits=w0,
                                  mid_waits={32: [(sLD, lw1)]})
                    prev_group = dD[1]
                    xres = av(a3, 24576 + (ot % 2) * 2048, F32, 128, 512)
                    pl.w("sync", sDV, sDV.n)
                    lx = ld(xres, xin_d[ot, :, tsl])
                    stg = av(a3, 28672 + (ot % 2) * 2048, F32, 128, 512)
                    pl.ws("vector", [dD, (sLD, lx)])
                    pl.w("vector", sST, sST.n)
                    pl.w("vector", sSC, sSC.n)
                    vv = pl.o("vector", lambda e, d=stg, x=xres, o=ot:
                              e.scalar_tensor_tensor(
                                  d, p3[:, :], b_wsde[:, o:o + 1], x,
                                  ALU.mult, ALU.add), inc=sDV)
                    pfree["p3"] = (sDV, vv)
                    pl.w("sync", sDV, vv)
                    st(xout_d[ot, :, tsl], stg)
                    if ffn1:
                        pl.w("vector", sTE, sTE.n)
                        vr = pl.o("vector", lambda e, s=stg: e.tensor_copy(
                            xr, s), inc=sDV)
                        pl.w("scalar", sDV, vv)
                        pl.w("scalar", sTE, sTE.n)
                        vq = pl.o("scalar", lambda e, s=stg: e.activation(
                            xq, s, AF.Square, bias=0.0, scale=1.0), inc=sSC)
                        pl.ws("tensor", [(sDV, vr), (sSC, vq)])
                        if ot == 0:
                            pl.ws("tensor", [pfree.get("pC0"),
                                             pfree.get("pC1")])
                        pl.o("tensor", lambda e, o=ot: e.matmul(
                            pC[0:1, 0:512], c_onec[:, 0:1], xr,
                            start=(o == 0), stop=(o == NT - 1)))
                        vmm = pl.o("tensor", lambda e, o=ot: e.matmul(
                            pC[0:1, 512:1024], c_onec[:, 0:1], xq,
                            start=(o == 0), stop=(o == NT - 1)), inc=sTE)
                if ffn1:
                    dep_rm = row_math_ln(pC[0:1, 0:512], pC[0:1, 512:1024],
                                         4 + sc, 6 + sc, comb_slot=3,
                                         dep=(sTE, vmm))
                    pfree["pC0"] = dep_rm
                    pfree["pC1"] = dep_rm
                    dA, dB = bcast2(rslot(4 + sc), rslot(3), dep_rm)
                    for kt in range(NT):
                        xres = av(a3, 24576 + (kt % 2) * 2048, F32, 128, 512)
                        pl.w("sync", sDV, sDV.n)
                        lx = ld(xres, xout_d[kt, :, tsl])
                        ustg = av(a3, 32768 + (kt % 2) * 1024, BF16, 128, 512)
                        pl.w("vector", sST, sST.n)
                        pl.ws("vector", [(sLD, lx), dA, dB])
                        pl.o("vector", lambda e, x=xres: e.tensor_tensor(
                            t1v, x, pB[:, 0:512], ALU.subtract))
                        vv = pl.o("vector", lambda e, d=ustg, k=kt:
                                  e.scalar_tensor_tensor(
                                      d, t1v, b_n1[:, k:k + 1],
                                      pB[:, 512:1024], ALU.mult, ALU.mult),
                                  inc=sDV)
                        pl.w("sync", sDV, vv)
                        st(unin[kt, :, tsl], ustg)
                    pfree["pB0"] = (sDV, sDV.n)
                    pfree["pB1"] = (sDV, sDV.n)

        ffn(x0_h, x1d, True, None, None)
        pl.w("gpsimd", sST, sST.n)
        vag = pl.o("gpsimd", lambda e: e.collective_compute(
            "AllGather", ALU.bypass, replica_groups=RG,
            ins=[unin[:, :, :]], outs=[unout[:, :, :, :]]), inc=sCC)

        barrier()

        vmem = pl.o("vector", lambda e: e.memset(carry[:, :], 0.0), inc=sDV)
        abuf = av(a2, 16384, F32, 128, 1024)
        ubuf = av(a2, 20480, F32, 128, 1024)
        hcb = av(a2, 24576, F32, 128, 1024)
        dtx2v = av(a2, 28672, F32, 128, 1024)
        vcc_prev = 0
        gp_last = 0
        p2pan = {0: 0, 1: 0}
        p2c = 0
        last_ip = 0
        for ch in range(4):
            rk, cc_ = ch // 2, ch % 2
            csl = slice(cc_ * 512, (cc_ + 1) * 512)
            rhs_last = None
            for kt in range(NT):
                rt = av(a2, kt * 1024, BF16, 128, 512)
                pl.w("sync", sTE, last_ip)
                pl.w("sync", sCC, vag)
                rhs_last = ld(rt, unout[rk, kt, :, csl])
            rhs_dep = (sLD, rhs_last)
            dtraw_dep = None
            zdep = None
            for ot in range(33):
                pan = wpanA if ot % 2 == 0 else wpanB
                pl.w("sync", sTE, p2pan[ot % 2])
                lw = ld(pan[:, :], iwpd[ot].rearrange("a p b -> p a b"))
                pk = "p0" if ot % 2 == 0 else "p1"
                pairs = []
                for kt in range(NT):
                    rt = av(a2, kt * 1024, BF16, 128, 512)
                    pairs.append((pan[:, kt * 128:(kt + 1) * 128], rt))
                dmm = mm_group(pk, pairs, waits=[(sLD, lw), rhs_dep])
                p2pan[ot % 2] = dmm[1]
                if ot == 32:
                    last_ip = dmm[1]
                if ot < 8:
                    dst = av(a3, ot * 2048, F32, 128, 512)
                elif ot < 16:
                    dst = av(a3, 16384 + (ot - 8) * 2048, F32, 128, 512)
                elif ot < 24:
                    dst = av(a3, 32768 + (ot - 16) * 2048, F32, 128, 512)
                elif ot == 24:
                    dst = av(a1, 32768, F32, 128, 512)
                else:
                    dst = None
                if dst is not None:
                    pl.ws("vector", [dmm])
                    vv = pl.o("vector", lambda e, d=dst, s=PSV[pk]:
                              e.tensor_copy(d, s), inc=sDV)
                    pfree[pk] = (sDV, vv)
                    if ot == 24:
                        db = av(a1, 34816, BF16, 128, 512)
                        vv2 = pl.o("vector", lambda e, d=db: e.tensor_copy(
                            d, av(a1, 32768, F32, 128, 512)), inc=sDV)
                        dtraw_dep = (sDV, vv2)
                else:
                    tau = ot - 25
                    zp = av(a1, 36864 + tau * 1024, BF16, 128, 512)
                    pl.ws("scalar", [dmm])
                    vs = pl.o("scalar", lambda e, d=zp, s=PSV[pk]:
                              e.activation(d, s, AF.Silu, bias=0.0,
                                           scale=1.0), inc=sSC)
                    pfree[pk] = (sSC, vs)
                    zdep = (sSC, vs)
            dt_deps = []
            for tau in range(8):
                pl.w("sync", sTE, p2c)
                lw = ld(wpanC[:, 0:128], dtpd[tau])
                db = av(a1, 34816, BF16, 128, 512)
                dmm = mm_group("p2", [(wpanC[:, 0:128], db)],
                               waits=[(sLD, lw), dtraw_dep])
                p2c = dmm[1]
                dtt = av(a1, tau * 2048, F32, 128, 512)
                pl.ws("scalar", [dmm])
                pl.w("scalar", sDV, sDV.n)
                pl.o("scalar", lambda e, d=dtt, s=PSV["p2"], t=tau:
                     e.activation(d, s, AF.Exp, bias=b_dtb[:, t:t + 1],
                                  scale=1.0))
                vs = pl.o("scalar", lambda e, d=dtt: e.activation(
                    d, d, AF.Ln, bias=1.0, scale=1.0), inc=sSC)
                pfree["p2"] = (sSC, vs)
                dt_deps.append((sSC, vs))
            last_bcast = 0
            for tau in range(8):
                dtt = av(a1, tau * 2048, F32, 128, 512)
                xrp = av(a3, 32768 + tau * 2048, F32, 128, 512)
                ytile = av(a1, 16384 + tau * 2048, F32, 128, 512)
                Ctf = av(a3, tau * 2048, F32, 128, 512)
                Btf = av(a3, 16384 + tau * 2048, F32, 128, 512)
                pl.w("vector", sTE, last_bcast)
                vstg = pl.o("vector", lambda e, s=Ctf: e.tensor_copy(ctr[:, :], s))
                vstg = pl.o("vector", lambda e, s=Btf: e.tensor_copy(btr[:, :], s),
                            inc=sDV)
                pl.ws("vector", [dt_deps[tau]])
                pl.o("vector", lambda e, d=dtt, x=xrp: e.tensor_tensor(
                    dtx2v[:, 0:512], d, x, ALU.mult))
                vdx = pl.o("vector", lambda e: e.tensor_copy(
                    dtx2v[:, 512:1024], dtx2v[:, 0:512]), inc=sDV)
                for np_ in range(16):
                    n0 = 2 * np_
                    ci = tau * 32 + n0
                    dBb = mm_group("pB0",
                                   [(c_eall[:, n0 * 128:(n0 + 1) * 128],
                                     btr[:, :])],
                                   waits=([(sDV, vdx), (sDV, vstg)]
                                          if np_ == 0 else []))
                    dBb2 = mm_group("pB1",
                                    [(c_eall[:, (n0 + 1) * 128:(n0 + 2) * 128],
                                      btr[:, :])])
                    dCb = mm_group("pC0",
                                   [(c_eall[:, n0 * 128:(n0 + 1) * 128],
                                     ctr[:, :])])
                    dCb2 = mm_group("pC1",
                                    [(c_eall[:, (n0 + 1) * 128:(n0 + 2) * 128],
                                      ctr[:, :])])
                    last_bcast = dCb2[1]
                    pl.w("scalar", sDV, sDV.n)
                    pl.o("scalar", lambda e, d=dtt, t=tau, n=n0: e.activation(
                        abuf[:, 0:512], d, AF.Exp, bias=0.0,
                        scale=b_acol[:, t * 32 + n:t * 32 + n + 1]))
                    va = pl.o("scalar", lambda e, d=dtt, t=tau, n=n0:
                              e.activation(abuf[:, 512:1024], d, AF.Exp,
                                           bias=0.0,
                                           scale=b_acol[:, t * 32 + n + 1:
                                                        t * 32 + n + 2]),
                              inc=sSC)
                    pl.ws("vector", [dBb, dBb2])
                    vu = pl.o("vector", lambda e: e.tensor_tensor(
                        ubuf[:, :], dtx2v[:, :], pB[:, :], ALU.mult), inc=sDV)
                    pfree["pB0"] = (sDV, vu)
                    pfree["pB1"] = (sDV, vu)
                    pl.w("vector", sSC, va)
                    pl.w("vector", sSC, vcc_prev)
                    pl.w("vector", sDV, vmem)
                    pl.o("vector", lambda e, c=ci: e.tensor_tensor_scan(
                        hbuf[:, 0:512], abuf[:, 0:512], ubuf[:, 0:512],
                        carry[:, c:c + 1], ALU.mult, ALU.add))
                    vh = pl.o("vector", lambda e, c=ci: e.tensor_tensor_scan(
                        hbuf[:, 512:1024], abuf[:, 512:1024],
                        ubuf[:, 512:1024], carry[:, c + 1:c + 2],
                        ALU.mult, ALU.add), inc=sDV)
                    pl.w("scalar", sDV, vh)
                    hsrc = bass_rust.AP(hbuf[:, :].tensor, 511,
                                        [[1024, 128], [512, 2]])
                    vcc_prev = pl.o("scalar", lambda e, c=ci, s=hsrc:
                                    e.activation(carry[:, c:c + 2], s, AF.Copy,
                                                 bias=0.0, scale=1.0), inc=sSC)
                    pl.ws("vector", [dCb, dCb2])
                    pl.w("vector", sGP, gp_last)
                    vhc = pl.o("vector", lambda e: e.tensor_tensor(
                        hcb[:, :], hbuf[:, :], pC[:, :], ALU.mult), inc=sDV)
                    pfree["pC0"] = (sDV, vhc)
                    pfree["pC1"] = (sDV, vhc)
                    pl.w("gpsimd", sDV, vhc)
                    if np_ == 0:
                        gp_last = pl.o("gpsimd", lambda e, y=ytile:
                                       e.tensor_tensor(y, hcb[:, 0:512],
                                                       hcb[:, 512:1024],
                                                       ALU.add), inc=sGP)
                    else:
                        pl.o("gpsimd", lambda e, y=ytile: e.tensor_tensor(
                            y, y, hcb[:, 0:512], ALU.add))
                        gp_last = pl.o("gpsimd", lambda e, y=ytile:
                                       e.tensor_tensor(y, y,
                                                       hcb[:, 512:1024],
                                                       ALU.add), inc=sGP)
                pl.w("vector", sGP, gp_last)
                vy = pl.o("vector", lambda e, y=ytile, x=xrp, t=tau:
                          e.scalar_tensor_tensor(y, x, b_dcol[:, t:t + 1],
                                                 y, ALU.mult, ALU.add),
                          inc=sDV)
                zp = av(a1, 36864 + tau * 1024, BF16, 128, 512)
                pl.w("scalar", sDV, vy)
                pl.ws("scalar", [zdep])
                vsz = pl.o("scalar", lambda e, z=zp: e.activation(
                    szb[:, :], z, AF.Copy, bias=0.0, scale=1.0), inc=sSC)
                v2t = av(a1, 49152 + tau * 1024, BF16, 128, 512)
                pl.w("vector", sSC, vsz)
                pl.o("vector", lambda e, d=v2t, y=ytile: e.tensor_tensor(
                    d, y, szb[:, :], ALU.mult), inc=sDV)
            v2_dep = (sDV, sDV.n)
            for ot in range(NT):
                pan = wpanA if ot % 2 == 0 else wpanB
                pl.w("sync", sTE, p2pan[ot % 2])
                lw = ld(pan[:, 0:1024], opjd[ot].rearrange("a p b -> p a b"))
                pk = "p0" if ot % 2 == 0 else "p1"
                pairs = []
                for tau in range(8):
                    v2t = av(a1, 49152 + tau * 1024, BF16, 128, 512)
                    pairs.append((pan[:, tau * 128:(tau + 1) * 128], v2t))
                dmm = mm_group(pk, pairs, waits=[(sLD, lw), v2_dep])
                p2pan[ot % 2] = dmm[1]
                stg = av(a1, 57344 + (ot % 2) * 2048, F32, 128, 512)
                pl.ws("vector", [dmm])
                pl.w("vector", sST, sST.n)
                vv = pl.o("vector", lambda e, d=stg, s=PSV[pk]:
                          e.tensor_copy(d, s), inc=sDV)
                pfree[pk] = (sDV, vv)
                pl.w("sync", sDV, vv)
                st(rsin[rk, ot, :, csl], stg)
        pl.w("gpsimd", sST, sST.n)
        vrs = pl.o("gpsimd", lambda e: e.collective_compute(
            "ReduceScatter", ALU.add, replica_groups=RG,
            ins=[rsin[:, :, :, :]], outs=[rsout[:, :, :]]), inc=sCC)

        barrier()

        rs2dep = {}
        for sc in range(2):
            tsl = slice(sc * 512, (sc + 1) * 512)
            dA, dB = bcast2(rslot(4 + sc), rslot(6 + sc), (sCC, vrs))
            s2key = "pC1" if sc == 0 else "p2"
            s2row = pC[0:1, 512:1024] if sc == 0 else p2[0:1, 0:512]
            vmm = None
            for kt in range(NT):
                x1t = av(a2, 16384 + (kt % 2) * 2048, F32, 128, 512)
                mot = av(a2, 20480 + (kt % 2) * 2048, F32, 128, 512)
                hht = av(a2, 24576 + (kt % 2) * 2048, F32, 128, 512)
                pl.w("sync", sDV, sDV.n)
                pl.w("sync", sSC, sSC.n)
                ld(x1t, x1d[kt, :, tsl])
                pl.w("sync", sCC, vrs)
                lm = ld(mot, rsout[kt, :, tsl])
                pl.ws("vector", [(sLD, lm), dA, dB])
                if kt >= 2:
                    pl.w("vector", sSC, sSC.n)
                pl.o("vector", lambda e, x=x1t, h=hht: e.tensor_tensor(
                    h, x, pB[:, 0:512], ALU.subtract))
                pl.o("vector", lambda e, h=hht: e.scalar_tensor_tensor(
                    h, h, 1.0, pB[:, 512:1024], ALU.mult, ALU.mult))
                vv = pl.o("vector", lambda e, h=hht, m=mot: e.tensor_tensor(
                    h, h, m, ALU.add), inc=sDV)
                xqr = statr[:, 512:1024]
                pl.w("scalar", sDV, vv)
                pl.w("scalar", sTE, sTE.n)
                vq = pl.o("scalar", lambda e, h=hht: e.activation(
                    xqr, h, AF.Square, bias=0.0, scale=1.0), inc=sSC)
                pl.ws("tensor", [(sSC, vq)])
                if kt == 0:
                    pl.ws("tensor", [pfree.get(s2key)])
                vmm = pl.o("tensor", lambda e, k=kt, o=s2row: e.matmul(
                    o, c_onec[:, 0:1], xqr, start=(k == 0),
                    stop=(k == NT - 1)), inc=sTE)
            drm = row_math_ln(None, s2row, None, 3, dep=(sTE, vmm),
                              rms_only=True)
            pfree[s2key] = drm
            dR = mm_group("pC0", [(c_oner[0:1, :], rslot(3))], waits=[drm])
            for kt in range(NT):
                x1t = av(a2, 16384 + (kt % 2) * 2048, F32, 128, 512)
                mot = av(a2, 20480 + (kt % 2) * 2048, F32, 128, 512)
                hht = av(a2, 24576 + (kt % 2) * 2048, F32, 128, 512)
                pl.w("sync", sDV, sDV.n)
                pl.w("sync", sSC, sSC.n)
                ld(x1t, x1d[kt, :, tsl])
                lm = ld(mot, rsout[kt, :, tsl])
                xt2 = (av(a1, 45056 + kt * 1024, BF16, 128, 512) if sc == 0
                       else av(a2, kt * 1024, BF16, 128, 512))
                pl.ws("vector", [(sLD, lm), dA, dB, dR])
                pl.o("vector", lambda e, x=x1t, h=hht: e.tensor_tensor(
                    h, x, pB[:, 0:512], ALU.subtract))
                pl.o("vector", lambda e, h=hht: e.scalar_tensor_tensor(
                    h, h, 1.0, pB[:, 512:1024], ALU.mult, ALU.mult))
                pl.o("vector", lambda e, h=hht, m=mot: e.tensor_tensor(
                    h, h, m, ALU.add))
                vv = pl.o("vector", lambda e, d=xt2, h=hht, k=kt:
                          e.scalar_tensor_tensor(d, h, b_n2[:, k:k + 1],
                                                 pC[:, 0:512], ALU.mult,
                                                 ALU.mult), inc=sDV)
            rs2dep[sc] = (sDV, sDV.n)
            pfree["pC0"] = rs2dep[sc]
            pfree["pB0"] = rs2dep[sc]
            pfree["pB1"] = rs2dep[sc]
        panfree = {}
        stgF3 = av(a2, 16384, F32, 128, 2048)
        castU3 = 0
        castG3 = 0
        pre_v = rs2dep[1][1]
        for ka in range(NKA):
            panU = wpanA if ka % 2 == 0 else wpanC
            panG = wpanB[:, :] if ka % 2 == 0 else av(a3, 45056, BF16, 128, 2048)
            if ka == 0:
                pl.w("sync", sDV, pre_v)
            pl.w("sync", sDV, castU3)
            lu = ld(stgF3, auu_h[ka].rearrange("a p b -> p a b"))
            pl.ws("vector", [(sLD, lu), (sTE, panfree.get(ka % 2, 0))])
            castU3 = pl.o("vector", lambda e, d=panU, s=stgF3:
                          e.tensor_copy(d[:, :], s), inc=sDV)
            pl.w("gpsimd", sTE, panfree.get(ka % 2, 0))
            lg = ld(panG[:, :], agu_h[ka].rearrange("a p b -> p a b"),
                    q="gpsimd")
            for sc in range(2):
                aren = a1 if sc == 0 else a3
                pkU = "p0" if sc == 0 else "p2"
                pkG = "p1" if sc == 0 else "p3"
                pairsU, pairsG = [], []
                for kt in range(NT):
                    xt2 = (av(a1, 45056 + kt * 1024, BF16, 128, 512) if sc == 0
                           else av(a2, kt * 1024, BF16, 128, 512))
                    pairsU.append((panU[:, kt * 128:(kt + 1) * 128], xt2))
                    pairsG.append((panG[:, kt * 128:(kt + 1) * 128], xt2))
                dU = mm_group(pkU, pairsU, waits=[(sDV, castU3), (sWC, lg),
                                                  rs2dep[sc]])
                dG = mm_group(pkG, pairsG)
                if sc == 1:
                    panfree[ka % 2] = dG[1]
                pl.ws("scalar", [dG])
                pl.w("scalar", sDV, sDV.n)
                vs = pl.o("scalar", lambda e, s=PSV[pkG]: e.activation(
                    szb[:, :], s, AF.Silu, bias=0.0, scale=1.0), inc=sSC)
                pfree[pkG] = (sSC, vs)
                vat = av(aren, ka * 1024, BF16, 128, 512)
                pl.ws("vector", [dU, (sSC, vs)])
                vv = pl.o("vector", lambda e, d=vat, u=PSV[pkU]:
                          e.tensor_tensor(d, u, szb[:, :], ALU.mult), inc=sDV)
                pfree[pkU] = (sDV, vv)
        vat_dep = (sDV, sDV.n)
        pl.w("sync", sSC, castG3)
        vmm3 = {0: None, 1: None}
        prev_pan = {0: 0, 1: 0}
        for ot in range(NT):
            pidx = ot % 2
            pan = (av(a1, 45056, BF16, 128, 44 * 128) if pidx == 0
                   else av(a2, 0, BF16, 128, 44 * 128))
            pl.w("gpsimd", sTE, max(prev_pan[pidx], sTE.n if ot < 2 else 0))
            lw = ld(pan, adn_h[ot].rearrange("a p b -> p a b"), q="gpsimd")
            for sc in range(2):
                aren = a1 if sc == 0 else a3
                pk = "p0" if sc == 0 else "p1"
                tsl = slice(sc * 512, (sc + 1) * 512)
                pairs = []
                for ka in range(NKA):
                    vat = av(aren, ka * 1024, BF16, 128, 512)
                    pairs.append((pan[:, ka * 128:(ka + 1) * 128], vat))
                dmm = mm_group(pk, pairs, waits=[(sWC, lw), vat_dep])
                prev_pan[pidx] = dmm[1]
                dA, dB = bcast2(rslot(4 + sc), rslot(6 + sc), dmm)
                x1t = av(a2, 16384 + sc * 2048, F32, 128, 512)
                mot = av(a2, 20480 + sc * 2048, F32, 128, 512)
                pl.w("sync", sDV, sDV.n)
                ld(x1t, x1d[ot, :, tsl])
                lm = ld(mot, rsout[ot, :, tsl])
                hht = av(a2, 24576 + sc * 2048, F32, 128, 512)
                stg = av(a1, 61440 + sc * 2048, F32, 128, 512)
                pl.ws("vector", [(sLD, lm), dA, dB, dmm])
                pl.w("vector", sST, sST.n)
                pl.w("vector", sSC, sSC.n)
                pl.w("vector", sTE, sTE.n)
                pl.o("vector", lambda e, x=x1t, h=hht: e.tensor_tensor(
                    h, x, pB[:, 0:512], ALU.subtract))
                pl.o("vector", lambda e, h=hht: e.scalar_tensor_tensor(
                    h, h, 1.0, pB[:, 512:1024], ALU.mult, ALU.mult))
                pl.o("vector", lambda e, h=hht, x=x1t: e.tensor_tensor(
                    h, h, x, ALU.add))
                pl.o("vector", lambda e, h=hht, m=mot: e.tensor_tensor(
                    h, h, m, ALU.add))
                vv = pl.o("vector", lambda e, d=stg, h=hht, s=PSV[pk]:
                          e.scalar_tensor_tensor(d, s, 1.0, h, ALU.mult,
                                                 ALU.add), inc=sDV)
                pfree[pk] = (sDV, vv)
                pfree["pB0"] = (sDV, vv)
                pfree["pB1"] = (sDV, vv)
                pl.w("sync", sDV, vv)
                st(x2d[ot, :, tsl], stg)
                s1r = pC[0:1, 0:512] if sc == 0 else p2[0:1, 0:512]
                s2r = pC[0:1, 512:1024] if sc == 0 else p3[0:1, 0:512]
                k1 = "pC0" if sc == 0 else "p2"
                k2_ = "pC1" if sc == 0 else "p3"
                xr3 = statr[:, 0:512]
                xq3 = statr[:, 512:1024]
                pl.w("vector", sTE, sTE.n)
                vr = pl.o("vector", lambda e, s=stg, d=xr3: e.tensor_copy(
                    d, s), inc=sDV)
                pl.w("scalar", sDV, vv)
                pl.w("scalar", sTE, sTE.n)
                vq = pl.o("scalar", lambda e, s=stg, d=xq3: e.activation(
                    d, s, AF.Square, bias=0.0, scale=1.0), inc=sSC)
                pl.ws("tensor", [(sDV, vr), (sSC, vq)])
                if ot == 0:
                    pl.ws("tensor", [pfree.get(k1), pfree.get(k2_)])
                pl.o("tensor", lambda e, o=ot, r=s1r: e.matmul(
                    r, c_onec[:, 0:1], xr3, start=(o == 0),
                    stop=(o == NT - 1)))
                vmm3[sc] = pl.o("tensor", lambda e, o=ot, r=s2r: e.matmul(
                    r, c_onec[:, 0:1], xq3, start=(o == 0),
                    stop=(o == NT - 1)), inc=sTE)
        d30 = row_math_ln(pC[0:1, 0:512], pC[0:1, 512:1024], 4, 6,
                          dep=(sTE, vmm3[0]))
        pfree["pC0"] = d30
        pfree["pC1"] = d30
        d31 = row_math_ln(p2[0:1, 0:512], p3[0:1, 0:512], 5, 7,
                          dep=(sTE, vmm3[1]))
        pfree["p2"] = d31
        pfree["p3"] = d31

        barrier()

        ffn(x2d, out_h, False, 4, 6)

        pl.w("sync", sST, sST.n)
        pl.w("sync", sCC, sCC.n)

        with nc.Block() as block:
            @block.sync
            def _(e):
                for f in pl.q["sync"]:
                    f(e)

            @block.tensor
            def _(e):
                for f in pl.q["tensor"]:
                    f(e)

            @block.vector
            def _(e):
                for f in pl.q["vector"]:
                    f(e)

            @block.scalar
            def _(e):
                for f in pl.q["scalar"]:
                    f(e)

            @block.gpsimd
            def _(e):
                for f in pl.q["gpsimd"]:
                    f(e)
    return nc


def _prep_shared(kw):
    H128 = _had(128).astype(np.float32)
    H64 = _had(64).astype(np.float32)
    hp = H128.astype(ml_dtypes.bfloat16)
    hn = (-H128).astype(ml_dtypes.bfloat16)
    hd = np.zeros((128, 128), np.float32)
    hd[:64, :64] = H64
    hd[64:, 64:] = H64
    hd = hd.astype(ml_dtypes.bfloat16)
    eall = np.zeros((128, 32 * 128), np.float32)
    for n in range(32):
        for p in range(128):
            eall[(p // 32) * 32 + n, n * 128 + p] = 1.0
    onec = np.ones((128, 1), np.float32)
    oner = np.ones((1, 128), np.float32)

    perm = np.empty(8192, np.int64)
    for kap in range(64):
        for q in range(2):
            for jt in range(64):
                perm[kap * 128 + q * 64 + jt] = jt * 128 + 2 * kap + q
    wu = np.ascontiguousarray(np.asarray(kw["L0_ffn_W_up"])[perm, :])
    wg = np.ascontiguousarray(np.asarray(kw["L0_ffn_W_gate"])[perm, :])
    fup = np.ascontiguousarray(wu.reshape(64, 128, 16, 128).transpose(0, 2, 3, 1))
    fga = np.ascontiguousarray(wg.reshape(64, 128, 16, 128).transpose(0, 2, 3, 1))
    wdt = np.ascontiguousarray(np.asarray(kw["L0_ffn_W_down"]).T
                               .reshape(64, 128, 2048))

    agu = np.ascontiguousarray(np.asarray(kw["L0_attn_gate_w"])
                               .reshape(44, 128, 16, 128).transpose(0, 2, 3, 1))
    auu = np.ascontiguousarray(np.asarray(kw["L0_attn_up_w"])
                               .reshape(44, 128, 16, 128).transpose(0, 2, 3, 1))
    adn = np.ascontiguousarray(np.asarray(kw["L0_attn_down_w"]).T
                               .reshape(44, 128, 16, 128).transpose(2, 0, 1, 3))

    sucol = np.ascontiguousarray(np.asarray(kw["L0_ffn_SU_ug"]).reshape(16, 128).T)
    wsug = np.asarray(kw["L0_ffn_Wscale_ug"])
    wsu = np.ascontiguousarray(wsug[:8192][perm].reshape(64, 128).T)
    wsg = np.ascontiguousarray(wsug[8192:][perm].reshape(64, 128).T)
    sud = np.ascontiguousarray(np.asarray(kw["L0_ffn_SU_dp"])[perm]
                               .reshape(64, 128).T)
    wsd = np.ascontiguousarray(np.asarray(kw["L0_ffn_Wscale_dp"])
                               .reshape(16, 128).T)
    n1c = np.ascontiguousarray(np.asarray(kw["L0_attn_norm1_w"])
                               .reshape(16, 128).T)
    n2c = np.ascontiguousarray(np.asarray(kw["L0_attn_norm2_w"])
                               .reshape(16, 128).T)
    cmisc = np.tile(np.array([EPS, 1.0, -0.5, -1.0], np.float32), (128, 1))
    f32 = np.float32
    return dict(hp=hp, hn=hn, hd=hd, eall=eall.astype(f32), onec=onec,
                oner=oner, fup=fup.astype(f32), fga=fga.astype(f32),
                wdt=wdt.astype(f32), agu=agu.astype(f32), auu=auu.astype(f32),
                adn=adn.astype(f32), sucol=sucol.astype(f32),
                wsu=wsu.astype(f32), wsg=wsg.astype(f32), sud=sud.astype(f32),
                wsd=wsd.astype(f32), n1c=n1c.astype(f32), n2c=n2c.astype(f32),
                cmisc=cmisc)


def _prep_half(kw, hk):
    W = np.asarray(kw["L0_attn_in_proj_w"])
    rows = []
    for tau in range(8):
        rows.extend(range(3072 + 1024 * hk + tau * 128,
                          3072 + 1024 * hk + tau * 128 + 128))
    for tau in range(8):
        for k in range(128):
            rows.append(2560 + (8 * hk + tau) * 32 + (k % 32))
    for tau in range(8):
        for k in range(128):
            rows.append(2048 + (8 * hk + tau) * 32 + (k % 32))
    rows.extend(range(5120, 5248))
    rows.extend(range(1024 * hk, 1024 * hk + 1024))
    Ws = W[np.array(rows), :]
    iwp = np.ascontiguousarray(Ws.reshape(33, 128, 16, 128).transpose(0, 2, 3, 1))

    dpw = np.asarray(kw["L0_attn_dt_proj_w"])[1024 * hk:1024 * hk + 1024]
    dtp = np.ascontiguousarray(dpw.reshape(8, 128, 128).transpose(0, 2, 1))
    dtb = np.ascontiguousarray(np.asarray(kw["L0_attn_dt_proj_b"])
                               [1024 * hk:1024 * hk + 1024].reshape(8, 128).T)
    alog = np.ascontiguousarray(np.asarray(kw["L0_attn_A_log"])
                                [1024 * hk:1024 * hk + 1024]
                                .reshape(8, 128, 32).transpose(1, 0, 2)
                                .reshape(128, 256))
    dcol = np.ascontiguousarray(np.asarray(kw["L0_attn_D_param"])
                                [1024 * hk:1024 * hk + 1024].reshape(8, 128).T)
    opw = np.asarray(kw["L0_attn_out_proj_w"])[:, 1024 * hk:1024 * hk + 1024]
    opj = np.ascontiguousarray(opw.T.reshape(8, 128, 16, 128)
                               .transpose(2, 0, 1, 3))
    f32 = np.float32
    return dict(iwp=iwp.astype(f32), dtp=dtp.astype(f32), dtb=dtb.astype(f32),
                alog=alog.astype(f32), dcol=dcol.astype(f32),
                opj=opj.astype(f32))


def kernel(**kw):
    shared = _prep_shared(kw)
    halves = [_prep_half(kw, 0), _prep_half(kw, 1)]
    x = np.asarray(kw["L0_attn_hidden_states"])
    nc = build_nc()
    in_maps = []
    for k in range(8):
        bk, hk = k // 2, k % 2
        x0 = np.ascontiguousarray(
            x[bk, 1024 * hk:1024 * (hk + 1), :].T).reshape(16, 128, 1024)
        m = dict(x0=np.ascontiguousarray(x0, dtype=np.float32))
        m.update(shared)
        m.update(halves[hk])
        in_maps.append(m)
    res = run_bass_kernel_spmd(nc, in_maps, core_ids=list(range(8)))
    out = np.zeros((4, 2048, 2048), np.float32)
    for k in range(8):
        bk, hk = k // 2, k % 2
        o = np.asarray(res.results[k]["out"]).reshape(2048, 1024)
        out[bk, 1024 * hk:1024 * (hk + 1), :] = o.T
    return out
